# version 12
# baseline (speedup 1.0000x reference)
import math
import contextlib
import numpy as np
import ml_dtypes

import concourse.bass as bass
import concourse.mybir as mybir
from concourse.alu_op_type import AluOpType
from concourse.bass import IndirectOffsetOnAxis
from concourse.bass_utils import run_bass_kernel_spmd

F32 = mybir.dt.float32
BF16 = mybir.dt.bfloat16
U32 = mybir.dt.uint32
I32 = mybir.dt.int32
AF = mybir.ActivationFunctionType
AL = AluOpType
bf16 = ml_dtypes.bfloat16
AX = mybir.AxisListType.X

ALL_PHASES = ("p0", "p1", "p2", "p3", "p4", "p5", "p5b", "p6", "p7")


class Cfg:
    def __init__(self, B=4, T=2048, D=2048, H=16, HK=4, Dh=128, WIN=1024,
                 E=4, HID=8192, n_cores=8, n_search=32):
        self.B, self.T, self.D, self.H, self.HK, self.Dh = B, T, D, H, HK, Dh
        self.WIN, self.E, self.HID, self.n_cores = WIN, E, HID, n_cores
        self.n_search = n_search
        self.N = B * T
        self.G = H // HK
        self.Tq = self.N // n_cores
        self.S = 2 * self.Tq
        self.CAP = self.N // E
        assert self.CAP // 2 == self.Tq
        self.DCH = D // 128
        self.NQC = self.Tq // 128
        self.NSC = self.S // 128
        self.KVW = HK * Dh
        self.QW = H * Dh
        self.ESL = min(512, self.QW)
        self.NES = self.QW // self.ESL
        self.ICW = min(512, self.Tq)
        self.NIC = self.Tq // self.ICW
        self.NJ = HID // 128
        self.JGRP = min(8, self.NJ)
        self.NGRP = self.NJ // self.JGRP
        self.NCH = self.N // 128
        self.LSLOT = self.CAP + 256
        assert self.LSLOT % 128 == 0
        self.WTAB = 2 * Dh
        assert self.ESL % self.WTAB == 0 and self.KVW % self.WTAB == 0
        self.THW = min(512, self.Tq)
        self.NTH = self.Tq // self.THW
        self.DSW = min(512, D)
        self.NDS = D // self.DSW
        self.AGCH = min(256, self.Tq)
        self.NAGC = self.Tq // self.AGCH
        self.PCW = min(2048, HID)
        self.NH1 = HID // self.PCW
        self.NW1U = self.DCH * self.NH1
        self.JPH = self.PCW // 128

    @classmethod
    def full(cls):
        return cls()

    @classmethod
    def small(cls):
        return cls(B=4, T=256, D=256, H=4, HK=2, Dh=128, WIN=128, E=4, HID=512)


class Emitter:
    ENGINES = ("sync", "vector", "scalar", "gpsimd", "tensor")

    def __init__(self, nc, stack):
        self.nc = nc
        self.stack = stack
        self.ops = {e: [] for e in self.ENGINES}
        self.sems = {}
        self.cnt = {}

    def sem(self, name):
        if name not in self.sems:
            self.sems[name] = self.stack.enter_context(self.nc.semaphore(name))
            self.cnt[name] = 0
        return self.sems[name]

    def op(self, engine, fn, waits=(), inc=None, dma=False):
        wl = []
        for w in waits:
            if isinstance(w, tuple):
                name, v = w
            else:
                name, v = w, self.cnt.get(w, 0)
            self.sem(name)
            if v > 0:
                wl.append((name, v))
        amt = 0
        if inc is not None:
            self.sem(inc)
            amt = 16 if dma else 1
            self.cnt[inc] = self.cnt.get(inc, 0) + amt
        self.ops[engine].append((fn, wl, inc, amt))

    def c(self, name):
        return self.cnt.get(name, 0)

    def materialize(self, block):
        for ename in self.ENGINES:
            mine = self.ops[ename]
            if not mine:
                continue
            deco = getattr(block, ename)

            def make(mine=mine):
                def body(eng):
                    for fn, wl, inc, amt in mine:
                        for sname, v in wl:
                            eng.wait_ge(self.sems[sname], v)
                        inst = fn(eng)
                        if inc is not None:
                            if inst is not None:
                                inst.then_inc(self.sems[inc], amt)
                            else:
                                eng.sem_inc(self.sems[inc], amt)
                return body
            deco(make())


class Arena:
    def __init__(self, nc, stack, name, cols, dt):
        self.t = stack.enter_context(nc.sbuf_tensor(name, [128, cols], dt))
        self.free_ranges = [(0, cols)]
        self.named = {}
        self.name = name
        self.hwm = 0

    def alloc(self, name, n, high=False):
        assert name not in self.named, name
        nr = (n + 15) // 16 * 16
        if high:
            for i in range(len(self.free_ranges) - 1, -1, -1):
                a, b = self.free_ranges[i]
                if b - a >= nr:
                    st = b - nr
                    if st == a:
                        self.free_ranges.pop(i)
                    else:
                        self.free_ranges[i] = (a, st)
                    self.named[name] = (st, nr)
                    self.hwm = max(self.hwm, max(a2 + n2 for a2, n2 in self.named.values()))
                    return self.t[:, st:st + n]
            raise RuntimeError(f"arena {self.name} OOM: {name} needs {n}, free={self.free_ranges}")
        for i, (a, b) in enumerate(self.free_ranges):
            if b - a >= nr:
                if a + nr == b:
                    self.free_ranges.pop(i)
                else:
                    self.free_ranges[i] = (a + nr, b)
                self.named[name] = (a, nr)
                self.hwm = max(self.hwm, max(a2 + n2 for a2, n2 in self.named.values()))
                return self.t[:, a:a + n]
        raise RuntimeError(f"arena {self.name} OOM: {name} needs {n}, free={self.free_ranges}")

    def freebuf(self, name):
        a, n = self.named.pop(name)
        self.free_ranges.append((a, a + n))
        self.free_ranges.sort()
        merged = []
        for a, b in self.free_ranges:
            if merged and merged[-1][1] == a:
                merged[-1][1] = b
            else:
                merged.append([a, b])
        self.free_ranges = [tuple(x) for x in merged]

    def view(self, name):
        a, n = self.named[name]
        return self.t[:, a:a + n]


class MK:
    def __init__(self, cfg: Cfg, taps=(), phases=ALL_PHASES):
        self.cfg = cfg
        self.taps = list(taps)
        self.phases = phases

    def build(self):
        cfg = self.cfg
        nc = bass.Bass("TRN2", target_bir_lowering=False)
        self.nc = nc
        c = cfg
        din = {}

        def inp(name, shape, dt=F32):
            din[name] = nc.dram_tensor(name, list(shape), dt, kind="ExternalInput").ap()

        inp("x_slab", [c.S, c.D])
        inp("wqT", [c.D, c.QW])
        inp("wkT", [c.D, c.KVW])
        inp("wvT", [c.D, c.KVW])
        inp("woT", [c.QW, c.D])
        inp("gqrep", [1, c.WTAB])
        inp("gkrep", [1, c.WTAB])
        inp("Wg", [c.D, c.E])
        inp("bgrep", [128, c.E])
        inp("W1H", [c.NW1U, 128, c.PCW])
        inp("W2H", [c.NJ, 128, c.D])
        inp("t0col", [128, 1])
        inp("rankcol", [128, 1])
        inp("hcol", [128, 1])
        inp("ecol", [128, 1])
        inp("kvalid", [128, c.NSC])
        self.din = din

        dsc = {}

        def scr(name, shape, dt, addr_space="Local"):
            dsc[name] = nc.dram_tensor(name, list(shape), dt, addr_space=addr_space).ap()

        scr("XT1", [c.S, c.D], BF16)
        scr("XQ", [c.Tq, c.QW], BF16)
        scr("XK", [c.S, c.KVW], BF16)
        scr("XT2loc", [c.Tq, c.D], BF16)
        scr("XT2allC", [c.N, c.D], BF16, addr_space="Shared")
        scr("LGloc", [c.Tq, c.E], F32)
        scr("LGall", [c.N, c.E], F32, addr_space="Shared")
        scr("RECd", [128 * c.NCH, 4], F32)
        scr("LISTloc", [c.E * c.LSLOT, 4], F32)
        scr("LISTall", [c.n_cores * c.E * c.LSLOT, 4], F32, addr_space="Shared")
        scr("LISTcmb", [c.E * c.LSLOT, 4], F32)
        scr("XTOK", [c.Tq, c.D], BF16)
        scr("YLOC", [c.Tq, c.D], BF16)
        scr("YALLC", [c.N, c.D], BF16, addr_space="Shared")
        scr("X1PARK", [c.Tq, c.D], F32)
        scr("X2FMDBG", [c.D, c.Tq], BF16)
        scr("W1BFH", [c.NW1U, 128, c.PCW], BF16)
        scr("W2BFH", [c.NJ, 128, c.D], BF16)
        self.dsc = dsc

        douts = {}
        douts["out"] = nc.dram_tensor("out", [c.Tq, c.D], F32, kind="ExternalOutput").ap()
        self.tapspec = {
            "xt1": ("XT1", [c.S, c.D], BF16),
            "xq": ("XQ", [c.Tq, c.QW], BF16),
            "xk": ("XK", [c.S, c.KVW], BF16),
            "x1": ("X1PARK", [c.Tq, c.D], F32),
            "xt2": ("XT2loc", [c.Tq, c.D], BF16),
            "lgall": ("LGall", [c.N, c.E], F32),
            "lgloc": ("LGloc", [c.Tq, c.E], F32),
            "x2fmdbg": ("X2FMDBG", [c.D, c.Tq], BF16),
            "rec": ("RECd", [128 * c.NCH, 4], F32),
            "listcmb": ("LISTcmb", [c.E * c.LSLOT, 4], F32),
            "xtok": ("XTOK", [c.Tq, c.D], BF16),
            "yloc": ("YLOC", [c.Tq, c.D], BF16),
            "yall": ("YALLC", [c.N, c.D], BF16),
        }
        for t in self.taps:
            sname, shape, dt = self.tapspec[t]
            douts["tap_" + t] = nc.dram_tensor("tap_" + t, list(shape), dt,
                                               kind="ExternalOutput").ap()
        self.douts = douts

        with contextlib.ExitStack() as stack:
            em = self.em = Emitter(nc, stack)
            self.af = Arena(nc, stack, "AF32", 25000, F32)
            self.ab = Arena(nc, stack, "ABF", 50000, BF16)
            self.ai = Arena(nc, stack, "AI32", max(600, self.cfg.WTAB + 64), I32)
            self.ps = [stack.enter_context(nc.psum_tensor(f"psb{i}", [128, 512], F32))
                       for i in range(8)]
            for ph in self.phases:
                getattr(self, "_" + ph)()
            self._taps()
            self._final()
            block = stack.enter_context(nc.Block())
            em.materialize(block)
        nc.finalize()
        return nc

    def dma(self, out, in_, waits=(), inc="ld"):
        self.em.op("sync", lambda s, out=out, in_=in_: s.dma_start(out=out, in_=in_),
                   waits=waits, inc=inc, dma=True)

    def dma_t(self, out, in_, waits=(), inc="ld"):
        self.em.op("sync", lambda s, out=out, in_=in_: s.dma_start(out=out, in_=in_,
                                                                   transpose=True),
                   waits=waits, inc=inc, dma=True)

    def vop(self, fn, waits=(), inc="ve"):
        self.em.op("vector", fn, waits=waits, inc=inc)

    def iota_f32(self, dst_f32, pattern, base, cm, waits=()):
        it = self.ai.view("iotatmp")
        npart = dst_f32.shape[0]
        n = dst_f32.shape[1]
        self.em.op("gpsimd", lambda g: g.iota(it[0:npart, 0:n], pattern=pattern, base=base,
                                              channel_multiplier=cm),
                   waits=list(waits) + [("iocast", self.em.c("iocast"))], inc="gp")
        self.em.op("vector", lambda v: v.tensor_copy(dst_f32, it[0:npart, 0:n]),
                   waits=[("gp", self.em.c("gp"))], inc="iocast")

    def f2i(self, dst_i32, src_f32, waits=(), inc="ve"):
        self.em.op("vector", lambda v: v.tensor_copy(dst_i32, src_f32), waits=waits, inc=inc)

    def dma_sl(self, stream, k, out, in_, waits=(), transpose=False):
        sem = f"{stream}{k % 2}"
        fn = (lambda s, out=out, in_=in_: s.dma_start(out=out, in_=in_, transpose=transpose))
        self.em.op("sync", fn, waits=waits, inc=sem, dma=True)

    def slw(self, stream, k):
        return (f"{stream}{k % 2}", 16 * (k // 2 + 1))

    def sl_full(self, stream):
        return [(f"{stream}0", self.em.c(f"{stream}0")),
                (f"{stream}1", self.em.c(f"{stream}1"))]

    def load_cast_cols(self, dst_bf, dram_cols_ap, w, nch=None, extra_guard=(), ngrp=4):
        em = self.em
        c = self.cfg
        if nch is None:
            nch = c.DCH
        ngrp = min(ngrp, nch)
        if not hasattr(self, "_wk"):
            self._wk = 0
        per = nch // ngrp
        for g in range(ngrp):
            k = self._wk
            slot = k % 2
            wl, wc = f"wl{slot}", f"wc{slot}"
            stg = self.wstg[:, slot * self.wstg_w:][:, 0:per * w]
            self.em.op("scalar", lambda s, g=g, stg=stg: s.dma_start(
                out=stg.rearrange("p (c j) -> p c j", c=per),
                in_=dram_cols_ap[g * per * 128:(g + 1) * per * 128, :].rearrange(
                    "(c p) j -> p c j", p=128)),
                waits=[(wc, em.c(wc))], inc=wl, dma=True)
            self.vop(lambda v, g=g, stg=stg: v.tensor_copy(
                dst_bf[:, g * per * w:(g + 1) * per * w], stg),
                waits=[(wl, em.c(wl))] + list(extra_guard), inc=wc)
            self._wk += 1

    def soft_barrier(self, exclude=()):
        em = self.em
        waits = [(s, em.c(s)) for s in list(em.cnt.keys())
                 if em.c(s) > 0 and s not in exclude]
        for eng in ("sync", "vector", "scalar", "gpsimd", "tensor"):
            em.op(eng, lambda e: None, waits=waits)

    def load_cast(self, dst_bf, dram_rows_ap, nch, w, extra_guard=()):
        em = self.em
        stg = self.wstg
        if not hasattr(self, "_wk"):
            self._wk = 0
        for dc in range(nch):
            k = self._wk
            slot = k % 2
            wl, wc = f"wl{slot}", f"wc{slot}"
            sv = stg[:, slot * self.wstg_w:][:, 0:w]
            self.em.op("scalar", lambda s, sv=sv, dc=dc: s.dma_start(
                out=sv, in_=dram_rows_ap[dc * 128:(dc + 1) * 128, :]),
                waits=[(wc, em.c(wc))], inc=wl, dma=True)
            self.vop(lambda v, sv=sv, dc=dc: v.tensor_copy(dst_bf[:, dc * w:(dc + 1) * w], sv),
                     waits=[(wl, em.c(wl))] + list(extra_guard), inc=wc)
            self._wk += 1

    def _floor_div(self, ef, srcf, q, scrf, scri, R):
        u = scrf
        R(lambda v: v.tensor_scalar(u, srcf, 1.0 / q, None, AL.mult))
        R(lambda v: v.tensor_copy(scri, u))
        R(lambda v: v.tensor_copy(ef, scri))
        R(lambda v: v.tensor_tensor(u, u, ef, AL.subtract))
        R(lambda v: v.tensor_scalar(u, u, 0.0, None, AL.is_lt))
        R(lambda v: v.tensor_tensor(ef, ef, u, AL.subtract))

    def _ag_remap(self, dstf, srcf, scrf3, scri, R):
        c = self.cfg
        w = srcf.shape[1]
        e1 = scrf3[:, 0:w]
        e2 = scrf3[:, w:2 * w]
        m1 = scrf3[:, 2 * w:3 * w]
        self._floor_div(e1, srcf, float(c.Tq), m1, scri, R)
        R(lambda v: v.scalar_tensor_tensor(m1, e1, -float(c.Tq), srcf, AL.mult, AL.add))
        self._floor_div(e2, m1, float(c.AGCH), dstf, scri, R)
        R(lambda v: v.scalar_tensor_tensor(
            dstf, e2, float((c.n_cores - 1) * c.AGCH), m1, AL.mult, AL.add))
        R(lambda v: v.scalar_tensor_tensor(dstf, e1, float(c.AGCH), dstf, AL.mult, AL.add))

    def _pc_in(self, k, q):
        src, dst, u, w = self.pc_units[k]
        s = k % 2
        waits = [(f"pcc{s}", k // 2)] if k >= 2 else []
        self.em.op(q, lambda e, u=u, s=s, w=w, src=src: e.dma_start(
            out=self.pcf[:, s * self.cfg.PCW:][:, 0:w], in_=self.din[src][u]),
            waits=waits, inc=f"pcl{s}", dma=True)

    def pc_step(self, n=1, q="scalar"):
        em = self.em
        for _ in range(n):
            k = self._pc_next
            if k >= len(self.pc_units):
                return
            while self._pc_next_in <= k + 1 and self._pc_next_in < len(self.pc_units):
                self._pc_in(self._pc_next_in, q)
                self._pc_next_in += 1
            src, dst, u, w = self.pc_units[k]
            s = k % 2
            waits = [(f"pcl{s}", 16 * (k // 2 + 1))]
            if k >= 2:
                waits.append((f"pcw{s}", 16 * (k // 2)))
            em.op("scalar", lambda e, s=s, w=w: e.activation(
                self.pcb[:, s * self.cfg.PCW:][:, 0:w],
                self.pcf[:, s * self.cfg.PCW:][:, 0:w], AF.Copy),
                waits=waits, inc=f"pcc{s}")
            em.op("gpsimd", lambda e, u=u, s=s, w=w, dst=dst: e.dma_start(
                out=self.dsc[dst][u], in_=self.pcb[:, s * self.cfg.PCW:][:, 0:w]),
                waits=[(f"pcc{s}", k // 2 + 1)], inc=f"pcw{s}", dma=True)
            self._pc_next = k + 1

    def _p0(self):
        c, em, af, ab = self.cfg, self.em, self.af, self.ab
        self.ai.alloc("iotatmp", self.cfg.WTAB)
        self.ones_bf = ab.alloc("ones_bf", 2)
        self.onesr_f = af.alloc("onesr_f", 128)
        self.halfpi = af.alloc("halfpi", 1)
        self.negBpad = af.alloc("negBpad", max(c.NSC, 2))
        self.epsln = af.alloc("epsln", 1)
        self.gq_bc = af.alloc("gq_bc", c.WTAB)
        self.gk_bc = af.alloc("gk_bc", c.WTAB)
        self.misc = af.alloc("misc", 1088)
        self.ones_f = af.alloc("ones_f", 1)
        self.wstg_w = max(c.D, c.DSW, 512)
        self.wstg = af.alloc("wstg", 2 * self.wstg_w)
        self.lnst = af.alloc("lnst", 32)
        self.myrec = af.alloc("myrec", c.NQC * 4)
        self.costab = af.alloc("costab", c.NSC * c.WTAB)
        self.sintab = af.alloc("sintab", c.NSC * c.WTAB)
        self.tri_bf = ab.alloc("tri_bf", 128)

        self.vop(lambda v: v.memset(self.ones_bf[:, :], 1.0))
        self.vop(lambda v: v.memset(self.ones_f[:, :], 1.0))
        self.vop(lambda v: v.memset(self.onesr_f[0:1, :], 1.0))
        self.vop(lambda v: v.memset(self.halfpi[:, :], math.pi / 2))
        self.vop(lambda v: v.memset(self.epsln[:, :], 1e-5))
        self.vop(lambda v: v.memset(self.tri_bf[:, :], 1.0))
        em.op("gpsimd", lambda g: g.affine_select(
            self.tri_bf[:, :], self.tri_bf[:, :], pattern=[[1, 128]],
            compare_op=AL.is_ge, fill=0.0, base=-1, channel_multiplier=-1),
            waits=["ve"], inc="gp")

        invf = af.alloc("invf", c.WTAB)
        self.iota_f32(invf[0:1, :], [[0, 4], [1, c.Dh // 2]], 0, 0)
        em.op("scalar", lambda s: s.activation(invf[0:1, :], invf[0:1, :], AF.Exp,
                                               scale=-2.0 * math.log(10000.0) / c.Dh),
              waits=["iocast"], inc="ac")
        invf_bc = af.alloc("invf_bc", c.WTAB)
        em.op("tensor", lambda t: t.matmul(self.ps[6][:, 0:c.WTAB], self.onesr_f[0:1, :],
                                           invf[0:1, :], start=True, stop=True),
              waits=["ac", "ve"], inc="mm")
        self.vop(lambda v: v.tensor_copy(invf_bc[:, :], self.ps[6][:, 0:c.WTAB]), waits=["mm"])

        gqr = af.alloc("gqr", c.WTAB)
        gkr = af.alloc("gkr", c.WTAB)
        self.dma(gqr[0:1, :], self.din["gqrep"], inc="ldgq")
        self.dma(gkr[0:1, :], self.din["gkrep"], inc="ldgk")
        em.op("tensor", lambda t: t.matmul(self.ps[6][:, 0:c.WTAB], self.onesr_f[0:1, :],
                                           gqr[0:1, :], start=True, stop=True),
              waits=["ldgq", "ve"], inc="mm")
        self.vop(lambda v: v.tensor_scalar(self.gq_bc[:, :], self.ps[6][:, 0:c.WTAB],
                                           1.0 / math.sqrt(c.Dh), None, AL.mult), waits=["mm"])
        em.op("tensor", lambda t: t.matmul(self.ps[6][:, 0:c.WTAB], self.onesr_f[0:1, :],
                                           gkr[0:1, :], start=True, stop=True),
              waits=["ve", "ldgk"], inc="mm")
        self.vop(lambda v: v.tensor_copy(self.gk_bc[:, :], self.ps[6][:, 0:c.WTAB]), waits=["mm"])

        bb = self.misc[:, 0:c.WTAB]
        self.vop(lambda v: v.tensor_tensor(bb[0:1, :], gqr[0:1, :], gkr[0:1, :], AL.mult),
                 waits=["ldgq", "ldgk"])
        em.op("scalar", lambda s: s.activation(bb[0:1, :], bb[0:1, :], AF.Abs),
              waits=["ve"], inc="ac")
        bmax = self.misc[:, c.WTAB:c.WTAB + 1]
        self.vop(lambda v: v.reduce_max(bmax[0:1, :], bb[0:1, 0:c.Dh], axis=AX), waits=["ac"])
        self.vop(lambda v: v.tensor_scalar(bmax[0:1, :], bmax[0:1, :],
                                           -1.0 / math.sqrt(c.Dh), None, AL.mult), waits=["ve"])
        em.op("tensor", lambda t: t.matmul(self.ps[6][:, 0:1], self.onesr_f[0:1, :],
                                           bmax[0:1, :], start=True, stop=True),
              waits=["ve"], inc="mm")
        negB = af.alloc("negB", 1)
        self.vop(lambda v: v.tensor_copy(negB[:, :], self.ps[6][:, 0:1]), waits=["mm"])
        kv_in = af.alloc("kv_in", c.NSC)
        self.dma(kv_in[:, :], self.din["kvalid"], inc="ldkv")
        self.vop(lambda v: v.tensor_scalar(self.negBpad[:, 0:c.NSC], kv_in[:, :],
                                           -1.0, 1e30, AL.add, AL.mult), waits=["ldkv", "ve"])
        self.vop(lambda v: v.tensor_scalar(self.negBpad[:, 0:c.NSC], self.negBpad[:, 0:c.NSC],
                                           negB[:, 0:1], None, AL.add), waits=["ve"])

        t0c = af.alloc("t0c", 1)
        self.dma(t0c[:, :], self.din["t0col"], inc="ldt0")
        scol = self.misc[:, 256:257]
        u = self.misc[:, 260:260 + c.WTAB]
        kfi = af.alloc("kfi", c.WTAB)
        kff = self.misc[:, 260 + c.WTAB:260 + 2 * c.WTAB]
        for sc in range(c.NSC):
            cosv = self.costab[:, sc * c.WTAB:(sc + 1) * c.WTAB]
            sinv = self.sintab[:, sc * c.WTAB:(sc + 1) * c.WTAB]
            self.iota_f32(scol, [[0, 1]], sc * 128, 1, waits=["ve"])
            self.vop(lambda v: v.tensor_tensor(scol, scol, t0c[:, :], AL.add),
                     waits=["iocast", "ldt0"])
            self.vop(lambda v, cosv=cosv: v.tensor_scalar(cosv, invf_bc[:, :], scol[:, 0:1],
                                                          None, AL.mult), waits=["ve"])
            def range_red(dst, bias025, wfirst):
                self.vop(lambda v: v.tensor_scalar(u, dst, 1.0 / (2 * math.pi),
                                                   bias025, AL.mult, AL.add), waits=wfirst)
                self.f2i(kfi.bitcast(I32), u, waits=["ve"])
                self.vop(lambda v: v.tensor_copy(kff, kfi.bitcast(I32)), waits=["ve"])
                self.vop(lambda v: v.scalar_tensor_tensor(dst, kff, -1.0, u,
                                                          AL.mult, AL.add), waits=["ve"])
                self.vop(lambda v: v.tensor_scalar(kff, dst, 0.5, None, AL.is_gt),
                         waits=["ve"])
                self.vop(lambda v: v.tensor_tensor(dst, dst, kff, AL.subtract), waits=["ve"])
                self.vop(lambda v: v.tensor_scalar(kff, dst, -0.5, None, AL.is_lt),
                         waits=["ve"])
                self.vop(lambda v: v.tensor_tensor(dst, dst, kff, AL.add), waits=["ve"])
            self.vop(lambda v, cosv=cosv, sinv=sinv: v.tensor_copy(sinv, cosv), waits=["ve"])
            range_red(sinv, 0.0, ["ve"])
            em.op("scalar", lambda s, sinv=sinv: s.activation(sinv, sinv, AF.Sin,
                                                              scale=2 * math.pi),
                  waits=["ve"], inc="ac")
            for h2 in range(2):
                self.vop(lambda v, sinv=sinv, h2=h2: v.tensor_scalar(
                    sinv[:, h2 * c.Dh:h2 * c.Dh + c.Dh // 2],
                    sinv[:, h2 * c.Dh:h2 * c.Dh + c.Dh // 2], -1.0, None, AL.mult),
                    waits=["ac"])
            range_red(cosv, 0.25, ["ac"])
            em.op("scalar", lambda s, cosv=cosv: s.activation(
                cosv, cosv, AF.Sin, scale=2 * math.pi), waits=["ve"], inc="ac")
        af.freebuf("invf")
        af.freebuf("invf_bc")
        af.freebuf("gqr")
        af.freebuf("gkr")
        af.freebuf("kv_in")
        af.freebuf("kfi")
        af.freebuf("t0c")
        af.freebuf("negB")
        self.pcf = af.alloc("pcstgf", 2 * c.PCW, high=True)
        self.pcb = ab.alloc("pcstgb", 2 * c.PCW, high=True)
        self.pc_units = ([("W1H", "W1BFH", u, c.PCW) for u in range(c.NW1U)] +
                         [("W2H", "W2BFH", u, c.D) for u in range(c.NJ)])
        self._pc_next_in = 0
        self._pc_next = 0

    def _jblocks(self, ic):
        c = self.cfg
        i0 = ic * c.ICW
        jlo = max(0, (c.Tq + i0 - c.WIN) // 128)
        jhi = min(c.NSC - 1, (c.Tq + i0 + c.ICW - 1) // 128)
        return [jb for jb in range(jlo, jhi + 1)
                if self._mask_kind(c.Tq + i0 - jb * 128) != "empty"]

    def _mask_kind(self, c0):
        c = self.cfg
        dmin, dmax = c0 - 127, c0 + c.ICW - 1
        if dmax < 0 or dmin > c.WIN:
            return "empty"
        if dmin >= 0 and dmax <= c.WIN:
            return "full"
        return "partial"

    def _p1(self):
        self.soft_barrier()
        c, em, af, ab = self.cfg, self.em, self.af, self.ab
        xstage = af.alloc("xstage", 2 * c.D)
        xlnbf = ab.alloc("xlnbf", 2 * c.D)
        st = self.lnst
        nbn = (c.D + 511) // 512
        for sc in range(c.NSC):
            xt = xstage[:, (sc % 2) * c.D:][:, 0:c.D]
            ob = xlnbf[:, (sc % 2) * c.D:][:, 0:c.D]
            self.dma_sl("xld", sc, xt, self.din["x_slab"][sc * 128:(sc + 1) * 128, :],
                        waits=[("ve", max(0, em.c("ve") - 2))] if sc >= 2 else [])
            for i in range(nbn):
                w = min(512, c.D - i * 512)
                self.vop(lambda v, xt=xt, i=i, w=w: v.bn_stats(
                    st[:, 6 * i:6 * i + 6], xt[:, i * 512:i * 512 + w]),
                    waits=[self.slw("xld", sc)])
            self.vop(lambda v: v.bn_aggr(st[:, 24:26], st[:, 0:6 * nbn]), waits=["ve"])
            em.op("scalar", lambda s: s.activation(st[:, 26:27], st[:, 25:26], AF.Sqrt,
                                                   bias=self.epsln[:, :]), waits=["ve"], inc="ac")
            self.vop(lambda v: v.reciprocal(st[:, 26:27], st[:, 26:27]), waits=["ac"])
            self.vop(lambda v: v.tensor_scalar(st[:, 27:28], st[:, 24:25], -1.0, None, AL.mult),
                     waits=["ve"])
            self.vop(lambda v, xt=xt, ob=ob: v.tensor_scalar(
                ob, xt, st[:, 27:28], st[:, 26:27], AL.add, AL.mult),
                waits=["ve"] + ([self.slw("xt1w", sc - 2)] if sc >= 2 else []))
            self.dma_sl("xt1w", sc, self.dsc["XT1"][sc * 128:(sc + 1) * 128, :], ob,
                        waits=[("ve", em.c("ve"))])
            self.pc_step(1)
        af.freebuf("xstage")
        ab.freebuf("xlnbf")

    def _p2(self):
        self.soft_barrier()
        c, em, af, ab = self.cfg, self.em, self.af, self.ab
        xfm = ab.alloc("xfm", c.DCH * c.Tq)
        wkv = ab.alloc("wkv", 2 * c.DCH * c.KVW)
        self.load_cast_cols(wkv[:, 0:c.DCH * c.KVW], self.din["wkT"], c.KVW)
        self.load_cast_cols(wkv[:, c.DCH * c.KVW:], self.din["wvT"], c.KVW)
        self.vtm = ab.alloc("vtm", c.NSC * c.KVW)
        af.alloc("ropebuf", c.ESL + 64)
        kstage = ab.alloc("kstage", 2 * c.KVW)
        for hf in range(2):
            nw = (hf + 1) * c.NQC
            hw = [("xt1w0", 16 * ((nw + 1) // 2)), ("xt1w1", 16 * (nw // 2))]
            for dc in range(c.DCH):
                self.dma_t(xfm[:, dc * c.Tq:(dc + 1) * c.Tq],
                           self.dsc["XT1"][hf * c.Tq:(hf + 1) * c.Tq, dc * 128:(dc + 1) * 128],
                           waits=hw + [("mm", em.c("mm"))],
                           inc="xfmr")
            for tcl in range(c.NQC):
                tc = hf * c.NQC + tcl
                for wi in range(2):
                    bank = self.ps[4 + (tc * 2 + wi) % 2]
                    for dc in range(c.DCH):
                        wl = []
                        if dc == 0:
                            wl = [("xfmr", em.c("xfmr")), ("wc0", em.c("wc0")),
                                  ("wc1", em.c("wc1")), ("bkfree", max(0, em.c("bkfree") - 1))] + (
                                  [self.slw("xkw", tc - 2)] if tc >= 2 else [])
                        em.op("tensor", lambda t, tcl=tcl, wi=wi, dc=dc, bank=bank: t.matmul(
                            bank[:, 0:c.KVW],
                            xfm[:, dc * c.Tq + tcl * 128:][:, 0:128],
                            wkv[:, wi * c.DCH * c.KVW + dc * c.KVW:][:, 0:c.KVW],
                            start=(dc == 0), stop=(dc == c.DCH - 1)), waits=wl, inc="mm")
                    if wi == 1:
                        self.vop(lambda v, tc=tc, bank=bank: v.tensor_copy(
                            self.vtm[:, tc * c.KVW:(tc + 1) * c.KVW], bank[:, 0:c.KVW]),
                            waits=[("mm", em.c("mm"))])
                        self.em.op("vector", lambda v: None, waits=["ve"], inc="bkfree")
                    else:
                        ko = kstage[:, (tc % 2) * c.KVW:][:, 0:c.KVW]
                        self._qk_epilogue(bank, ko, tc, is_q=False)
                        self.dma_sl("xkw", tc, self.dsc["XK"][tc * 128:(tc + 1) * 128, :], ko,
                                    waits=[("epid", em.c("epid"))])
                    self.pc_step(1)
        ab.freebuf("wkv")
        wq = ab.alloc("wq", c.DCH * c.ESL)
        qstage = ab.alloc("qstage", 2 * c.ESL)
        for es in range(c.NES):
            self.load_cast_cols(wq, self.din["wqT"][:, es * c.ESL:(es + 1) * c.ESL], c.ESL,
                                extra_guard=[("mm", em.c("mm"))])
            for tc in range(c.NQC):
                bank = self.ps[4 + tc % 2]
                sc = c.NSC - c.NQC + tc
                for dc in range(c.DCH):
                    wl = []
                    kq = es * c.NQC + tc
                    if dc == 0:
                        wl = [("wc0", em.c("wc0")), ("wc1", em.c("wc1")), ("bkfree", max(0, em.c("bkfree") - 1))] + (
                             [self.slw("xqw", kq - 2)] if kq >= 2 else [])
                    em.op("tensor", lambda t, tc=tc, dc=dc, bank=bank: t.matmul(
                        bank[:, 0:c.ESL],
                        xfm[:, dc * c.Tq + tc * 128:][:, 0:128],
                        wq[:, dc * c.ESL:(dc + 1) * c.ESL],
                        start=(dc == 0), stop=(dc == c.DCH - 1)), waits=wl, inc="mm")
                qo = qstage[:, (tc % 2) * c.ESL:][:, 0:c.ESL]
                self._qk_epilogue(bank, qo, sc, is_q=True)
                self.dma_sl("xqw", es * c.NQC + tc,
                            self.dsc["XQ"][tc * 128:(tc + 1) * 128, es * c.ESL:(es + 1) * c.ESL],
                            qo, waits=[("epid", em.c("epid"))])
                self.pc_step(1)
        ab.freebuf("wq")
        ab.freebuf("kstage")
        ab.freebuf("qstage")
        ab.freebuf("xfm")
        af.freebuf("ropebuf")
        af.freebuf("costab")
        af.freebuf("sintab")

    def _qk_epilogue(self, bank, obf, sc, is_q):
        c, em = self.cfg, self.em
        W = c.ESL if is_q else c.KVW
        nh = W // c.Dh
        rope = self.af.view("ropebuf")
        rbuf = rope[:, 0:W]
        ss = rope[:, W:W + nh]
        inv = rope[:, W + 16:W + 16 + nh]
        gbc = self.gq_bc if is_q else self.gk_bc
        cosv = self.costab[:, sc * c.WTAB:(sc + 1) * c.WTAB]
        sinv = self.sintab[:, sc * c.WTAB:(sc + 1) * c.WTAB]
        tmp = self.misc[:, 0:c.Dh // 2]
        ng2 = W // c.WTAB
        for g2 in range(ng2):
            src = bank[:, g2 * c.WTAB:(g2 + 1) * c.WTAB]
            dst = rbuf[:, g2 * c.WTAB:(g2 + 1) * c.WTAB]
            self.vop(lambda v, src=src, dst=dst: v.tensor_tensor(dst, src, cosv, AL.mult),
                     waits=[("mm", em.c("mm")), ("epid", em.c("epid"))])
            for h2 in range(2):
                for half in range(2):
                    sh = bank[:, g2 * c.WTAB + h2 * c.Dh + (1 - half) * (c.Dh // 2):][:, 0:c.Dh // 2]
                    sv = sinv[:, h2 * c.Dh + half * (c.Dh // 2):][:, 0:c.Dh // 2]
                    dv = dst[:, h2 * c.Dh + half * (c.Dh // 2):][:, 0:c.Dh // 2]
                    last_read = (g2 == ng2 - 1 and h2 == 1 and half == 1)
                    self.vop(lambda v, sh=sh, sv=sv: v.tensor_tensor(tmp, sh, sv, AL.mult),
                             waits=["ve"])
                    self.vop(lambda v, dv=dv: v.tensor_tensor(dv, dv, tmp, AL.add), waits=["ve"])
                    if last_read:
                        self.em.op("vector", lambda v: None, waits=["ve"], inc="bkfree")
        for h in range(nh):
            em.op("scalar", lambda s, h=h: s.activation(
                self.misc[:, 64 + (h % 2) * c.Dh:64 + (h % 2 + 1) * c.Dh],
                rbuf[:, h * c.Dh:(h + 1) * c.Dh], AF.Square,
                accum_out=ss[:, h:h + 1]), waits=["ve", ("ac", max(0, self.em.c("ac") - 1))],
                inc="ac")
        em.op("scalar", lambda s: s.activation(inv[:, 0:nh], ss[:, 0:nh], AF.Sqrt),
              waits=["ac"], inc="ac")
        self.vop(lambda v: v.tensor_scalar(inv[:, 0:nh], inv[:, 0:nh], 1e-6, None, AL.add),
                 waits=["ac"])
        self.vop(lambda v: v.reciprocal(inv[:, 0:nh], inv[:, 0:nh]), waits=["ve"])
        for h in range(nh):
            gv = gbc[:, (h % 2) * c.Dh:(h % 2 + 1) * c.Dh]
            self.vop(lambda v, h=h, gv=gv: v.scalar_tensor_tensor(
                obf[:, h * c.Dh:(h + 1) * c.Dh], rbuf[:, h * c.Dh:(h + 1) * c.Dh],
                inv[:, h:h + 1], gv, AL.mult, AL.mult), waits=["ve"])
        self.em.op("vector", lambda v: None, waits=["ve"], inc="epid")

    def _p3(self):
        self.soft_barrier()
        c, em, ab, af = self.cfg, self.em, self.ab, self.af
        hpc0 = 128 // self.cfg.Dh
        self.yh = ab.alloc("yh", (c.H // hpc0) * c.Tq)
        self.wgb = ab.alloc("wgb", c.DCH * c.E)
        self.load_cast(self.wgb, self.din["Wg"], c.DCH, c.E)
        self.bgb = af.alloc("bgb", c.E)
        self.dma(self.bgb[:, :], self.din["bgrep"], inc="ldbg")
        self.mask_of = {}
        for ic in range(c.NIC):
            for jb in self._jblocks(ic):
                c0 = c.Tq + ic * c.ICW - jb * 128
                if self._mask_kind(c0) == "partial" and c0 not in self.mask_of:
                    self.mask_of[c0] = len(self.mask_of)
        self.maskbuf = ab.alloc("masks", max(len(self.mask_of), 1) * c.ICW)
        for c0, mi in self.mask_of.items():
            mv = self.maskbuf[:, mi * c.ICW:(mi + 1) * c.ICW]
            self.vop(lambda v, mv=mv: v.memset(mv, 1.0))
            em.op("gpsimd", lambda g, mv=mv, c0=c0: g.affine_select(
                mv, mv, pattern=[[1, c.ICW]], compare_op=AL.is_ge, fill=0.0,
                base=c0, channel_multiplier=-1), waits=[("ve", em.c("ve"))], inc="gp")
            em.op("gpsimd", lambda g, mv=mv, c0=c0: g.affine_select(
                mv, mv, pattern=[[-1, c.ICW]], compare_op=AL.is_ge, fill=0.0,
                base=c.WIN - c0, channel_multiplier=1), waits=["gp"], inc="gp")
        qfm = ab.alloc("qfm", 2 * c.G * c.Tq)
        kfm = ab.alloc("kfm", 2 * c.S)
        hpc = 128 // c.Dh
        self.hpc = hpc
        self.rbcbuf = af.alloc("rbcbuf", 2 * c.ICW)
        ebuf = ab.alloc("ebuf", 4 * c.ICW)
        first = True
        mme_after = []
        epi_val = {}
        pend_epi = [None]
        prev_flush = [0]
        for hk in range(c.HK):
            wstage = [("mmE", mme_after[hk - 2])] if hk >= 2 else []
            kfv = kfm[:, (hk % 2) * c.S:][:, 0:c.S]
            qfv = qfm[:, (hk % 2) * c.G * c.Tq:][:, 0:c.G * c.Tq]
            self.dma_t(kfv[0:c.Dh, :], self.dsc["XK"][:, hk * c.Dh:(hk + 1) * c.Dh],
                       waits=self.sl_full("xkw") + wstage, inc=f"kfr{hk % 2}")
            for g in range(c.G):
                h = hk * c.G + g
                self.dma_t(qfv[0:c.Dh, g * c.Tq:(g + 1) * c.Tq],
                           self.dsc["XQ"][:, h * c.Dh:(h + 1) * c.Dh],
                           waits=self.sl_full("xqw") + wstage, inc=f"qfr{hk % 2}")
            for ic in range(c.NIC):
                jbs = self._jblocks(ic)
                njb = len(jbs)
                for g in range(c.G):
                    h = hk * c.G + g
                    gi = (hk * c.NIC + ic) * c.G + g
                    ps_y = self.ps[2 if gi % 2 == 0 else 5]
                    ps_r = self.ps[3 if gi % 2 == 0 else 7]
                    qv = qfv[0:c.Dh, g * c.Tq + ic * c.ICW:][:, 0:c.ICW]
                    consume_spec = {}
                    mme_blk = {}

                    def dnpv(k, jbs=jbs, njb=njb, ps_r=ps_r, ps_y=ps_y, gi=gi, hk=hk,
                             consume_spec=consume_spec, mme_blk=mme_blk):
                        jb = jbs[k]
                        ev = ebuf[:, (k % 4) * c.ICW:][:, 0:c.ICW]
                        wl = [consume_spec[k]]
                        if k == 0 and gi >= 2:
                            wl.append(("ve2b", epi_val[gi - 2]))
                        em.op("tensor", lambda t, ev=ev, k=k, njb=njb, ps_r=ps_r: t.matmul(
                            ps_r[0:1, 0:c.ICW], self.ones_bf[:, 0:1], ev,
                            start=(k == 0), stop=(k == njb - 1)), waits=wl, inc="mmE")
                        vv = self.vtm[:, jb * c.KVW + hk * c.Dh:][:, 0:c.Dh]
                        em.op("tensor", lambda t, ev=ev, vv=vv, k=k, njb=njb, ps_y=ps_y: t.matmul(
                            ps_y[0:c.Dh, 0:c.ICW], vv, ev,
                            start=(k == 0), stop=(k == njb - 1)), inc="mmE")
                        mme_blk[k] = em.c("mmE")

                    for bi, jb in enumerate(jbs):
                        sb = self.ps[(0, 1, 4)[bi % 3]]
                        ev = ebuf[:, (bi % 4) * c.ICW:][:, 0:c.ICW]
                        kv = kfv[0:c.Dh, jb * 128:][:, 0:128]
                        wl = [("ac2", max(0, em.c("ac2") - 2))]
                        if bi == 0:
                            wl += [(f"qfr{hk % 2}", em.c(f"qfr{hk % 2}")),
                                   (f"kfr{hk % 2}", em.c(f"kfr{hk % 2}"))]
                        if first:
                            wl += [("gp", em.c("gp")), ("ve", em.c("ve"))]
                            first = False
                        em.op("tensor", lambda t, sb=sb, kv=kv, qv=qv: t.matmul(
                            sb[:, 0:c.ICW], kv, qv, start=True, stop=True),
                            waits=wl, inc="mm")
                        ebguard = ([("mmE", mme_blk[bi - 4])] if bi >= 4
                                   else [("mmE", prev_flush[0])])
                        em.op("scalar", lambda s, sb=sb, ev=ev, jb=jb: s.activation(
                            ev, sb[:, 0:c.ICW], AF.Exp, bias=self.negBpad[:, jb:jb + 1]),
                            waits=[("mm", em.c("mm"))] + ebguard, inc="ac2")
                        c0 = c.Tq + ic * c.ICW - jb * 128
                        if self._mask_kind(c0) == "partial":
                            mi = self.mask_of[c0]
                            em.op("vector", lambda v, ev=ev, mi=mi: v.tensor_tensor(
                                ev, ev, self.maskbuf[:, mi * c.ICW:(mi + 1) * c.ICW], AL.mult),
                                waits=[("ac2", em.c("ac2"))], inc="ve2")
                            consume_spec[bi] = ("ve2", em.c("ve2"))
                        else:
                            consume_spec[bi] = ("ac2", em.c("ac2"))
                        if bi == 2 and pend_epi[0] is not None:
                            pend_epi[0]()
                            pend_epi[0] = None
                        if bi >= 2:
                            dnpv(bi - 2)
                    if pend_epi[0] is not None:
                        pend_epi[0]()
                        pend_epi[0] = None
                    for k in range(max(0, njb - 2), njb):
                        dnpv(k)
                    prev_flush[0] = em.c("mmE")
                    rrow = (self.misc[0:1, 0:c.ICW] if gi % 2 == 0
                            else self.misc[0:1, 544:544 + c.ICW])
                    rbc = self.rbcbuf[:, (gi % 2) * c.ICW:][:, 0:c.ICW]
                    em.op("vector", lambda v, rrow=rrow, ps_r=ps_r: v.reciprocal(
                        rrow, ps_r[0:1, 0:c.ICW]), waits=[("mmE", em.c("mmE"))],
                        inc="ve2b")
                    ve2b_after_recip = em.c("ve2b")
                    yo = self.yh[(h % hpc) * c.Dh:(h % hpc + 1) * c.Dh,
                                 (h // hpc) * c.Tq + ic * c.ICW:][:, 0:c.ICW]

                    def epi(gi=gi, rrow=rrow, rbc=rbc, yo=yo, ps_y=ps_y,
                            var=ve2b_after_recip):
                        em.op("tensor", lambda t, rrow=rrow: t.matmul(
                            self.ps[6][:, 0:c.ICW], self.onesr_f[0:1, :], rrow,
                            start=True, stop=True), waits=[("ve2b", var)], inc="mmb")
                        em.op("vector", lambda v, rbc=rbc: v.tensor_copy(
                            rbc, self.ps[6][:, 0:c.ICW]), waits=[("mmb", em.c("mmb"))],
                            inc="ve2b")
                        em.op("vector", lambda v, yo=yo, rbc=rbc, ps_y=ps_y: v.tensor_tensor(
                            yo, ps_y[0:c.Dh, 0:c.ICW], rbc[0:c.Dh, :], AL.mult),
                            waits=[("ve2b", em.c("ve2b"))], inc="ve2b")
                        epi_val[gi] = em.c("ve2b")
                    pend_epi[0] = epi
            mme_after.append(em.c("mmE"))
        if pend_epi[0] is not None:
            pend_epi[0]()
            pend_epi[0] = None
        ab.freebuf("ebuf")
        ab.freebuf("qfm")
        ab.freebuf("kfm")
        ab.freebuf("vtm")
        ab.freebuf("masks")
        af.freebuf("rbcbuf")

    def _p4(self):
        self.soft_barrier()
        c, em, ab, af = self.cfg, self.em, self.ab, self.af
        NEC = c.QW // 128
        wo = ab.alloc("wo", NEC * c.DSW)
        xjit = af.alloc("xjit", 2 * c.DSW)
        x1st = af.alloc("x1st", 2 * c.DSW)
        for ds in range(c.NDS):
            self.load_cast_cols(wo, self.din["woT"][:, ds * c.DSW:(ds + 1) * c.DSW], c.DSW,
                                nch=NEC, extra_guard=[("mm", em.c("mm"))])
            for tc in range(c.NQC):
                bank = self.ps[4 + tc % 2]
                kx = ds * c.NQC + tc
                xj = xjit[:, (tc % 2) * c.DSW:][:, 0:c.DSW]
                self.dma_sl("xjl", kx, xj,
                            self.din["x_slab"][c.Tq + tc * 128:c.Tq + (tc + 1) * 128,
                                               ds * c.DSW:(ds + 1) * c.DSW],
                            waits=[("x1ve", max(0, em.c("x1ve") - 1))])
                for ec in range(NEC):
                    wl = []
                    if ec == 0:
                        wl = [("wc0", em.c("wc0")), ("wc1", em.c("wc1")),
                              ("ve2b", em.c("ve2b")),
                              ("x1ve", max(0, em.c("x1ve") - 1))]
                    em.op("tensor", lambda t, tc=tc, ec=ec, bank=bank: t.matmul(
                        bank[:, 0:c.DSW],
                        self.yh[:, ec * c.Tq + tc * 128:][:, 0:128],
                        wo[:, ec * c.DSW:(ec + 1) * c.DSW],
                        start=(ec == 0), stop=(ec == NEC - 1)), waits=wl, inc="mm")
                xo = x1st[:, (tc % 2) * c.DSW:][:, 0:c.DSW]
                em.op("vector", lambda v, xj=xj, xo=xo, bank=bank: v.tensor_tensor(
                    xo, xj, bank[:, 0:c.DSW], AL.add),
                    waits=[("mm", em.c("mm")), self.slw("xjl", kx)] + (
                          [self.slw("x1pw", kx - 2)] if kx >= 2 else []), inc="x1ve")
                self.dma_sl("x1pw", kx, self.dsc["X1PARK"][tc * 128:(tc + 1) * 128,
                                                           ds * c.DSW:(ds + 1) * c.DSW], xo,
                            waits=[("x1ve", em.c("x1ve"))])
        ab.freebuf("wo")
        af.freebuf("xjit")
        af.freebuf("x1st")
        ab.freebuf("yh")

    def _p5(self):
        self.soft_barrier()
        c, em = self.cfg, self.em
        ab, af = self.ab, self.af
        xlnbf = ab.alloc("x2bf", 2 * c.D)
        x1rd = af.alloc("x1rd", 2 * c.D)
        st = self.lnst
        nbn = (c.D + 511) // 512
        for tc in range(c.NQC):
            xt = x1rd[:, (tc % 2) * c.D:][:, 0:c.D]
            ob = xlnbf[:, (tc % 2) * c.D:][:, 0:c.D]
            self.dma_sl("x1rd", tc, xt, self.dsc["X1PARK"][tc * 128:(tc + 1) * 128, :],
                        waits=[self.slw("x1pw", (c.NDS - 1) * c.NQC + tc),
                               ("ve", max(0, em.c("ve") - 4))])
            for i in range(nbn):
                w = min(512, c.D - i * 512)
                self.vop(lambda v, xt=xt, i=i, w=w: v.bn_stats(
                    st[:, 6 * i:6 * i + 6], xt[:, i * 512:i * 512 + w]),
                    waits=[self.slw("x1rd", tc)])
            self.vop(lambda v: v.bn_aggr(st[:, 24:26], st[:, 0:6 * nbn]), waits=["ve"])
            em.op("scalar", lambda s: s.activation(st[:, 26:27], st[:, 25:26], AF.Sqrt,
                                                   bias=self.epsln[:, :]), waits=["ve"], inc="ac")
            self.vop(lambda v: v.reciprocal(st[:, 26:27], st[:, 26:27]), waits=["ac"])
            self.vop(lambda v: v.tensor_scalar(st[:, 27:28], st[:, 24:25], -1.0, None, AL.mult),
                     waits=["ve"])
            self.vop(lambda v, xt=xt, ob=ob: v.tensor_scalar(
                ob, xt, st[:, 27:28], st[:, 26:27], AL.add, AL.mult),
                waits=["ve"] + ([self.slw("xt2w", tc - 2)] if tc >= 2 else []))
            self.dma_sl("xt2w", tc, self.dsc["XT2loc"][tc * 128:(tc + 1) * 128, :], ob,
                        waits=[("ve", em.c("ve"))])
        x2fm = ab.alloc("x2fm", 2 * c.DCH * 128)
        ab.freebuf("x2bf")
        af.freebuf("x1rd")
        tpc = c.AGCH // 128
        for cch in range(c.NAGC):
            rows = c.n_cores * c.AGCH
            em.op("gpsimd", lambda g, cch=cch, rows=rows: g.collective_compute(
                "AllGather", AL.bypass, replica_groups=[list(range(c.n_cores))],
                ins=[self.dsc["XT2loc"][cch * c.AGCH:(cch + 1) * c.AGCH, :]],
                outs=[self.dsc["XT2allC"][cch * rows:(cch + 1) * rows, :]]),
                waits=[self.slw("xt2w", cch * tpc + i) for i in range(tpc)], inc="cc")
        wgb = self.wgb
        bgb = self.bgb
        lg = af.alloc("lgbuf", c.NQC * c.E)
        for tc in range(c.NQC):
            xv = x2fm[:, (tc % 2) * c.DCH * 128:][:, 0:c.DCH * 128]
            x2sem = f"x2fr{tc % 2}"
            for dc in range(c.DCH):
                self.em.op("sync", lambda s, xv=xv, dc=dc, tc=tc: s.dma_start(
                    out=xv[:, dc * 128:(dc + 1) * 128],
                    in_=self.dsc["XT2loc"][tc * 128:(tc + 1) * 128, dc * 128:(dc + 1) * 128],
                    transpose=True),
                    waits=[self.slw("xt2w", tc),
                           ("gmm", max(0, em.c("gmm") - c.DCH))],
                    inc=x2sem, dma=True)
            if "x2fmdbg" in self.taps:
                for dc in range(c.DCH):
                    self.dma(self.dsc["X2FMDBG"][dc * 128:(dc + 1) * 128,
                                                 tc * 128:(tc + 1) * 128],
                             xv[:, dc * 128:(dc + 1) * 128],
                             waits=[(x2sem, em.c(x2sem))], inc="dbgw")
            for dc in range(c.DCH):
                wl = [(x2sem, em.c(x2sem)), ("wc0", em.c("wc0")), ("wc1", em.c("wc1")),
                      ("lgve", em.c("lgve"))] if dc == 0 else []
                if dc == 0 and "x2fmdbg" in self.taps:
                    wl.append(("dbgw", em.c("dbgw")))
                em.op("tensor", lambda t, xv=xv, dc=dc: t.matmul(
                    self.ps[7][:, 0:c.E], xv[:, dc * 128:(dc + 1) * 128],
                    wgb[:, dc * c.E:(dc + 1) * c.E],
                    start=(dc == 0), stop=(dc == c.DCH - 1)), waits=wl, inc="gmm")
            self.vop(lambda v, tc=tc: v.tensor_tensor(
                lg[:, tc * c.E:(tc + 1) * c.E], self.ps[7][:, 0:c.E], bgb[:, :], AL.add),
                waits=[("gmm", em.c("gmm")), ("ldbg", em.c("ldbg"))], inc="lgve")
        self.dma(self.dsc["LGloc"].rearrange("(a p) e -> p a e", p=128),
                 lg[:, :].rearrange("p (a e) -> p a e", e=c.E),
                 waits=[("lgve", em.c("lgve"))], inc="lgw")
        em.op("gpsimd", lambda g: g.collective_compute(
            "AllGather", AL.bypass, replica_groups=[list(range(c.n_cores))],
            ins=[self.dsc["LGloc"]], outs=[self.dsc["LGall"]]),
            waits=[("lgw", em.c("lgw"))], inc="cc")
        ab.freebuf("wgb")
        ab.freebuf("x2fm")
        af.freebuf("lgbuf")
        af.freebuf("bgb")

    def _p5b(self):
        self.soft_barrier(exclude=("cc",))
        c, em, ab, af = self.cfg, self.em, self.ab, self.af
        E, NCH = c.E, c.NCH
        P = af.alloc("planes", NCH * E)
        self.dma(P[:, :].rearrange("p (ch e) -> p ch e", e=E),
                 self.dsc["LGall"].rearrange("(ch p) e -> p ch e", p=128),
                 waits=[("cc", em.c("cc"))], inc="pld")
        le = [P[:, e:NCH * E:E] for e in range(E)]
        Wk = af.alloc("rwork", NCH * 33 + 8 * E + 32)
        off = [0]

        def buf(n):
            v = Wk[:, off[0]:off[0] + n]
            off[0] += n
            return v
        m1, m2, u, p1, p2, tmp, seen = (buf(NCH) for _ in range(7))
        first1 = [buf(NCH) for _ in range(E)]
        first2 = [buf(NCH) for _ in range(E)]
        score = [buf(NCH) for _ in range(E)]

        def R(fn, waits=None):
            em.op("vector", fn, waits=waits if waits is not None else [("rve", em.c("rve"))],
                  inc="rve")
        R(lambda v: v.tensor_tensor(m1, le[0], le[1], AL.max), waits=[("pld", em.c("pld"))])
        for e in range(2, E):
            R(lambda v, e=e: v.tensor_tensor(m1, m1, le[e], AL.max))
        R(lambda v: v.memset(seen, 0.0))
        for e in range(E):
            R(lambda v, e=e: v.tensor_tensor(first1[e], le[e], m1, AL.is_equal))
            R(lambda v: v.tensor_scalar(tmp, seen, -1.0, 1.0, AL.mult, AL.add))
            R(lambda v, e=e: v.tensor_tensor(first1[e], first1[e], tmp, AL.mult))
            R(lambda v, e=e: v.tensor_tensor(seen, seen, first1[e], AL.add))
        R(lambda v: v.memset(m2, -1e30))
        for e in range(E):
            R(lambda v, e=e: v.scalar_tensor_tensor(score[e], first1[e], -1e30, le[e],
                                                    AL.mult, AL.add))
            R(lambda v, e=e: v.tensor_tensor(m2, m2, score[e], AL.max))
        R(lambda v: v.memset(seen, 0.0))
        for e in range(E):
            R(lambda v, e=e: v.tensor_tensor(first2[e], score[e], m2, AL.is_equal))
            R(lambda v: v.tensor_scalar(tmp, seen, -1.0, 1.0, AL.mult, AL.add))
            R(lambda v, e=e: v.tensor_tensor(first2[e], first2[e], tmp, AL.mult))
            R(lambda v, e=e: v.tensor_tensor(seen, seen, first2[e], AL.add))
        R(lambda v: v.tensor_tensor(u, m2, m1, AL.subtract))
        em.op("scalar", lambda s: s.activation(u, u, AF.Exp),
              waits=[("rve", em.c("rve"))], inc="rac")
        R(lambda v: v.tensor_scalar(p1, u, 1.0, None, AL.add), waits=[("rac", em.c("rac"))])
        R(lambda v: v.reciprocal(p1, p1))
        R(lambda v: v.tensor_tensor(p2, u, p1, AL.mult))
        for e in range(E):
            R(lambda v, e=e: v.tensor_tensor(score[e], first1[e], p1, AL.mult))
            R(lambda v, e=e: v.tensor_tensor(tmp, first2[e], p2, AL.mult))
            R(lambda v, e=e: v.tensor_tensor(score[e], score[e], tmp, AL.add))
            R(lambda v, e=e: v.tensor_tensor(tmp, first1[e], first2[e], AL.add))
            R(lambda v, e=e: v.tensor_scalar(tmp, tmp, -1.0, 1.0, AL.mult, AL.add))
            R(lambda v, e=e: v.scalar_tensor_tensor(score[e], tmp, -1.0, score[e],
                                                    AL.mult, AL.add))
        lo, hi, tt, cnt, ge, tb = (buf(E) for _ in range(6))
        maskb = ab.alloc("selmaskb", NCH * E)
        mb = [maskb[:, e * NCH:(e + 1) * NCH] for e in range(E)]
        R(lambda v: v.memset(lo[0:1, :], 0.0))
        R(lambda v: v.memset(hi[0:1, :], 1.0))
        for it in range(c.n_search):
            R(lambda v: v.tensor_tensor(tt[0:1, :], lo[0:1, :], hi[0:1, :], AL.add))
            R(lambda v: v.tensor_scalar(tt[0:1, :], tt[0:1, :], 0.5, None, AL.mult))
            em.op("tensor", lambda t: t.matmul(self.ps[6][:, 0:E], self.onesr_f[0:1, :],
                                               tt[0:1, :], start=True, stop=True),
                  waits=[("rve", em.c("rve"))], inc="rmm")
            R(lambda v: v.tensor_copy(tb, self.ps[6][:, 0:E]), waits=[("rmm", em.c("rmm"))])
            for e in range(E):
                R(lambda v, e=e: v.tensor_scalar(mb[e], score[e], tb[:, e:e + 1], 0.0,
                                                 AL.is_ge, AL.add,
                                                 accum_out=cnt[:, e:e + 1]))
            em.op("tensor", lambda t: t.matmul(self.ps[6][0:1, 0:E], self.ones_f[:, 0:1],
                                               cnt, start=True, stop=True),
                  waits=[("rve", em.c("rve"))], inc="rmm")
            R(lambda v: v.tensor_scalar(ge[0:1, :], self.ps[6][0:1, 0:E], float(c.CAP),
                                        None, AL.is_ge), waits=[("rmm", em.c("rmm"))])
            R(lambda v: v.tensor_tensor(cnt[0:1, :], tt[0:1, :], lo[0:1, :], AL.subtract))
            R(lambda v: v.tensor_tensor(cnt[0:1, :], cnt[0:1, :], ge[0:1, :], AL.mult))
            R(lambda v: v.tensor_tensor(lo[0:1, :], lo[0:1, :], cnt[0:1, :], AL.add))
            R(lambda v: v.tensor_tensor(cnt[0:1, :], tt[0:1, :], hi[0:1, :], AL.subtract))
            R(lambda v: v.tensor_scalar(ge[0:1, :], ge[0:1, :], -1.0, 1.0, AL.mult, AL.add))
            R(lambda v: v.tensor_tensor(cnt[0:1, :], cnt[0:1, :], ge[0:1, :], AL.mult))
            R(lambda v: v.tensor_tensor(hi[0:1, :], hi[0:1, :], cnt[0:1, :], AL.add))
            self.pc_step(1)
        em.op("tensor", lambda t: t.matmul(self.ps[6][:, 0:E], self.onesr_f[0:1, :],
                                           lo[0:1, :], start=True, stop=True),
              waits=[("rve", em.c("rve"))], inc="rmm")
        R(lambda v: v.tensor_copy(tb, self.ps[6][:, 0:E]), waits=[("rmm", em.c("rmm"))])
        rpl = [buf(NCH) for _ in range(E)]
        mpl = [buf(NCH) for _ in range(E)]
        ctot = buf(NCH)
        for e in range(E):
            R(lambda v, e=e: v.tensor_scalar(mb[e], score[e], tb[:, e:e + 1], None, AL.is_ge))
            em.op("tensor", lambda t, e=e: t.matmul(self.ps[6][:, 0:NCH], self.tri_bf[:, :],
                                                    mb[e], start=True, stop=True),
                  waits=[("rve", em.c("rve"))], inc="rmm")
            em.op("tensor", lambda t, e=e: t.matmul(self.ps[7][0:1, 0:NCH],
                                                    self.ones_bf[:, 0:1], mb[e],
                                                    start=True, stop=True), inc="rmm")
            R(lambda v, e=e: v.tensor_copy(rpl[e], self.ps[6][:, 0:NCH]),
              waits=[("rmm", em.c("rmm"))])
            R(lambda v: v.tensor_copy(ctot[0:1, :], self.ps[7][0:1, 0:NCH]))
            sh = 1
            while sh < NCH:
                R(lambda v, sh=sh: v.tensor_tensor(ctot[0:1, sh:NCH], ctot[0:1, sh:NCH],
                                                   ctot[0:1, 0:NCH - sh], AL.add))
                sh *= 2
            em.op("tensor", lambda t: t.matmul(self.ps[6][:, 0:NCH], self.onesr_f[0:1, :],
                                               ctot[0:1, 0:NCH], start=True, stop=True),
                  waits=[("rve", em.c("rve"))], inc="rmm")
            R(lambda v, e=e: v.tensor_tensor(rpl[e][:, 1:NCH], rpl[e][:, 1:NCH],
                                             self.ps[6][:, 0:NCH - 1], AL.add),
              waits=[("rmm", em.c("rmm"))])
            R(lambda v, e=e: v.tensor_scalar(mpl[e], rpl[e], float(c.CAP), None, AL.is_lt))
            R(lambda v, e=e: v.tensor_tensor(mpl[e], mpl[e], mb[e], AL.mult))
            self.pc_step(2)
        ab.freebuf("selmaskb")
        rec = buf(4 * NCH)
        s0 = rec[:, 0:4 * NCH:4]
        q0 = rec[:, 1:4 * NCH:4]
        s1 = rec[:, 2:4 * NCH:4]
        q1 = rec[:, 3:4 * NCH:4]
        for (sv, qv, fl, pv) in ((s0, q0, first1, p1), (s1, q1, first2, p2)):
            R(lambda v, sv=sv: v.memset(sv, 0.0))
            R(lambda v, qv=qv: v.memset(qv, 0.0))
            for e in range(E):
                R(lambda v, e=e, fl=fl: v.tensor_tensor(tmp, rpl[e], fl[e], AL.mult))
                R(lambda v, e=e, fl=fl: v.scalar_tensor_tensor(u, fl[e], float(c.CAP * e),
                                                               tmp, AL.mult, AL.add))
                R(lambda v, sv=sv: v.tensor_tensor(sv, sv, u, AL.add))
                R(lambda v, e=e, fl=fl: v.tensor_tensor(tmp, fl[e], mpl[e], AL.mult))
                R(lambda v, pv=pv: v.tensor_tensor(tmp, tmp, pv, AL.mult))
                R(lambda v, qv=qv: v.tensor_tensor(qv, qv, tmp, AL.add))
                self.pc_step(1)
        self.dma(self.dsc["RECd"].rearrange("(p ch) r -> p (ch r)", p=128), rec[:, 0:4 * NCH],
                 waits=[("rve", em.c("rve"))], inc="recw")
        rk = af.alloc("rkcol", 4)
        self.dma(rk[:, 0:1], self.din["rankcol"], inc="ldrk")
        self.dma(rk[:, 1:2], self.din["hcol"], inc="ldrk2")
        self.dma(rk[:, 2:3], self.din["ecol"], inc="ldrk3")
        pio = af.alloc("pio", 1)
        self.iota_f32(pio[:, 0:1], [[0, 1]], 0, 1, waits=[("rve", em.c("rve"))])
        idxf = af.alloc("idxf", c.NQC + 8)
        idxu = self.ai.alloc("idxu", max(c.NQC, 2))
        R(lambda v: v.tensor_scalar(idxf[:, 0:1], pio[:, 0:1], float(NCH), None, AL.mult),
          waits=[("iocast", em.c("iocast")), ("ldrk", em.c("ldrk")), ("ldrk2", em.c("ldrk2")), ("ldrk3", em.c("ldrk3"))])
        R(lambda v: v.scalar_tensor_tensor(idxf[:, 0:1], rk[:, 0:1], float(c.NQC),
                                           idxf[:, 0:1], AL.mult, AL.add))
        for k in range(c.NQC):
            R(lambda v, k=k: v.tensor_scalar(idxf[:, 1 + k:2 + k], idxf[:, 0:1], float(k),
                                             None, AL.add))
            self.f2i(idxu[:, k:k + 1], idxf[:, 1 + k:2 + k],
                     waits=[("rve", em.c("rve"))], inc="rve")
            em.op("gpsimd", lambda g, k=k: g.indirect_dma_start(
                out=self.myrec[:, 4 * k:4 * k + 4], out_offset=None, in_=self.dsc["RECd"],
                in_offset=IndirectOffsetOnAxis(ap=idxu[:, k:k + 1].bitcast(U32), axis=0)),
                waits=[("rve", em.c("rve")), ("recw", em.c("recw"))], inc="ind", dma=True)
        neg1 = self.misc[:, 0:(c.E * c.LSLOT * 4) // 128]
        R(lambda v, neg1=neg1: v.memset(neg1, -1.0))
        self.dma(self.dsc["LISTloc"].rearrange("(p a) r -> p (a r)", p=128), neg1,
                 waits=[("rve", em.c("rve"))], inc="linit")
        tokb = af.alloc("tokb", 1)
        R(lambda v: v.scalar_tensor_tensor(tokb[:, 0:1], rk[:, 0:1], float(c.Tq), pio[:, 0:1],
                                           AL.mult, AL.add))
        sc_idx = self.ai.alloc("scidx", 2)
        sval = af.alloc("sval", 8)
        for j in range(2):
            for k in range(c.NQC):
                sv = self.myrec[:, 4 * k + 2 * j:4 * k + 2 * j + 1]
                qv = self.myrec[:, 4 * k + 2 * j + 1:4 * k + 2 * j + 2]
                R(lambda v, qv=qv: v.tensor_scalar(sval[:, 0:1], qv, 0.0, None, AL.is_gt),
                  waits=[("rve", em.c("rve")), ("ind", em.c("ind")),
                         ("lsc", em.c("lsc")), ("linit", em.c("linit"))])
                R(lambda v, sv=sv: v.tensor_tensor(sval[:, 1:2], sv, sval[:, 0:1], AL.mult))
                R(lambda v: v.tensor_scalar(sval[:, 4:5], sval[:, 1:2], 1.0 / c.CAP,
                                            None, AL.mult))
                self.f2i(sc_idx[:, 0:1], sval[:, 4:5], waits=[("rve", em.c("rve"))], inc="rve")
                R(lambda v: v.tensor_copy(sval[:, 6:7], sc_idx[:, 0:1]))
                R(lambda v: v.tensor_tensor(sval[:, 7:8], sval[:, 4:5], sval[:, 6:7],
                                            AL.subtract))
                R(lambda v: v.tensor_scalar(sval[:, 7:8], sval[:, 7:8], 0.0, None, AL.is_lt))
                R(lambda v: v.tensor_tensor(sval[:, 4:5], sval[:, 6:7], sval[:, 7:8],
                                            AL.subtract))
                R(lambda v: v.scalar_tensor_tensor(sval[:, 1:2], sval[:, 4:5],
                                                   float(c.LSLOT - c.CAP), sval[:, 1:2],
                                                   AL.mult, AL.add))
                R(lambda v: v.tensor_tensor(sval[:, 1:2], sval[:, 1:2], sval[:, 0:1], AL.mult))
                R(lambda v: v.tensor_scalar(sval[:, 2:3], sval[:, 0:1], -1.0, 1.0,
                                            AL.mult, AL.add))
                R(lambda v: v.tensor_scalar(sval[:, 3:4], pio[:, 0:1],
                                            float(c.E * c.LSLOT - 128), None, AL.add))
                R(lambda v: v.tensor_tensor(sval[:, 2:3], sval[:, 2:3], sval[:, 3:4], AL.mult))
                R(lambda v: v.tensor_tensor(sval[:, 1:2], sval[:, 1:2], sval[:, 2:3], AL.add))
                self.f2i(sc_idx[:, 0:1], sval[:, 1:2], waits=[("rve", em.c("rve"))], inc="rve")
                R(lambda v, k=k: v.tensor_scalar(sval[:, 5:6], tokb[:, 0:1], float(k * 128),
                                                 None, AL.add))
                R(lambda v: v.memset(sval[:, 6:8], 0.0))
                R(lambda v: v.tensor_copy(sval[:, 4:5], sval[:, 5:6]))
                em.op("gpsimd", lambda g: g.indirect_dma_start(
                    out=self.dsc["LISTloc"],
                    out_offset=IndirectOffsetOnAxis(ap=sc_idx[:, 0:1].bitcast(U32), axis=0),
                    in_=sval[:, 4:8], in_offset=None),
                    waits=[("rve", em.c("rve")), ("linit", em.c("linit"))],
                    inc="lsc", dma=True)
        em.op("gpsimd", lambda g: g.collective_compute(
            "AllGather", AL.bypass, replica_groups=[list(range(c.n_cores))],
            ins=[self.dsc["LISTloc"]], outs=[self.dsc["LISTall"]]),
            waits=[("lsc", em.c("lsc"))], inc="cc")
        ncol = c.E * c.LSLOT * 4 // 128
        comb = af.alloc("comb", 2 * ncol)
        ca, cb = comb[:, 0:ncol], comb[:, ncol:2 * ncol]
        self.dma(ca, self.dsc["LISTall"][0:c.E * c.LSLOT, :].rearrange(
            "(p a) q -> p (a q)", p=128), waits=[("cc", em.c("cc"))], inc="pld2")
        for r in range(1, c.n_cores):
            base = r * c.E * c.LSLOT
            self.dma(cb, self.dsc["LISTall"][base:base + c.E * c.LSLOT, :].rearrange(
                "(p a) q -> p (a q)", p=128),
                waits=[("pld2", em.c("pld2")), ("rve", em.c("rve"))], inc="pld2")
            R(lambda v: v.tensor_tensor(ca, ca, cb, AL.max), waits=[("pld2", em.c("pld2"))])
        self.dma(self.dsc["LISTcmb"].rearrange("(p a) q -> p (a q)", p=128), ca,
                 waits=[("rve", em.c("rve"))], inc="lcw")
        af.freebuf("comb")
        myl = af.alloc("mylist", c.NQC * 4)
        R(lambda v: v.scalar_tensor_tensor(idxf[:, 0:1], rk[:, 2:3], float(c.LSLOT),
                                           pio[:, 0:1], AL.mult, AL.add))
        R(lambda v: v.scalar_tensor_tensor(idxf[:, 0:1], rk[:, 1:2], float(c.Tq),
                                           idxf[:, 0:1], AL.mult, AL.add))
        tokid = self.ai.alloc("tokid", c.NQC)
        for k in range(c.NQC):
            R(lambda v, k=k: v.tensor_scalar(idxf[:, 1 + k:2 + k], idxf[:, 0:1],
                                             float(k * 128), None, AL.add))
            self.f2i(idxu[:, k:k + 1], idxf[:, 1 + k:2 + k],
                     waits=[("rve", em.c("rve"))], inc="rve")
            em.op("gpsimd", lambda g, k=k: g.indirect_dma_start(
                out=myl[:, 4 * k:4 * k + 4], out_offset=None, in_=self.dsc["LISTcmb"],
                in_offset=IndirectOffsetOnAxis(ap=idxu[:, k:k + 1].bitcast(U32), axis=0)),
                waits=[("rve", em.c("rve")), ("lcw", em.c("lcw"))], inc="ind", dma=True)
        for k in range(c.NQC):
            R(lambda v, k=k: v.tensor_scalar(idxf[:, 1 + k:2 + k],
                                             myl[:, 4 * k:4 * k + 1], 0.0, None, AL.max),
              waits=[("ind", em.c("ind"))])
        rmf = af.alloc("rmf", 4 * c.NQC)
        self._ag_remap(rmf[:, 0:c.NQC], idxf[:, 1:1 + c.NQC],
                       rmf[:, c.NQC:4 * c.NQC], tokid[:, 0:c.NQC], R)
        self.f2i(tokid[:, 0:c.NQC], rmf[:, 0:c.NQC],
                 waits=[("rve", em.c("rve"))], inc="rve")
        af.freebuf("rmf")
        grow = ab.alloc("growbuf", 2 * c.D)
        for k in range(c.NQC):
            gv = grow[:, (k % 2) * c.D:][:, 0:c.D]
            em.op("gpsimd", lambda g, k=k, gv=gv: g.indirect_dma_start(
                out=gv, out_offset=None, in_=self.dsc["XT2allC"],
                in_offset=IndirectOffsetOnAxis(ap=tokid[:, k:k + 1].bitcast(U32), axis=0)),
                waits=[("rve", em.c("rve")), ("cc", em.c("cc"))] +
                      ([(f"xtokw{k % 2}", 16 * (k // 2))] if k >= 2 else []),
                inc=f"gind{k % 2}", dma=True)
            self.dma(self.dsc["XTOK"][k * 128:(k + 1) * 128, :], gv,
                     waits=[(f"gind{k % 2}", 16 * (k // 2 + 1))], inc=f"xtokw{k % 2}")
        ab.freebuf("growbuf")
        self.tokfm = ab.alloc("tokfm", c.DCH * c.Tq)
        for dc in range(c.DCH):
            self.dma_t(self.tokfm[:, dc * c.Tq:(dc + 1) * c.Tq],
                       self.dsc["XTOK"][:, dc * 128:(dc + 1) * 128],
                       waits=[("xtokw0", em.c("xtokw0")), ("xtokw1", em.c("xtokw1"))],
                       inc="tokfr")
        af.freebuf("planes")
        af.freebuf("rwork")
        af.freebuf("mylist")
        af.freebuf("idxf")
        af.freebuf("rkcol")
        af.freebuf("pio")
        af.freebuf("tokb")
        af.freebuf("sval")
        self.pc_step(len(self.pc_units) - self._pc_next)
        assert self._pc_next == len(self.pc_units)
        af.freebuf("pcstgf")
        ab.freebuf("pcstgb")
        self.w1b = ab.alloc("w1b", 4 * c.DCH * 128)
        self.w2b = ab.alloc("w2b", c.JGRP * c.D)
        guards = [("pcw0", em.c("pcw0")), ("pcw1", em.c("pcw1")),
                  ("xtokw0", em.c("xtokw0")), ("xtokw1", em.c("xtokw1")),
                  ("rve", em.c("rve")),
                  ("ind", em.c("ind"))]
        for j in range(min(4, c.NJ)):
            self._ffn_w1_load(j, guards)
        for j in range(min(c.JGRP, c.NJ)):
            self._ffn_w2_load(j, guards)

    def _ffn_w1_load(self, j, waits_extra=()):
        c, em = self.cfg, self.em
        wb = self.w1b[:, (j % 4) * c.DCH * 128:][:, 0:c.DCH * 128]
        h1, jq = j // c.JPH, j % c.JPH
        em.op("scalar", lambda s, wb=wb, h1=h1, jq=jq: s.dma_start(
            out=wb.rearrange("p (c j) -> p c j", c=c.DCH),
            in_=self.dsc["W1BFH"][h1 * c.DCH:(h1 + 1) * c.DCH, :,
                                  jq * 128:(jq + 1) * 128].rearrange("c p j -> p c j")),
            waits=[("ffmm", max(0, em.c("ffmm") - 3 * c.DCH * c.NTH))] + list(waits_extra),
            inc=f"w1l{j % 4}", dma=True)

    def _ffn_w2_load(self, j, waits_extra=()):
        c, em = self.cfg, self.em
        jj = j % c.JGRP
        em.op("gpsimd", lambda g, jj=jj, j=j: g.dma_start(
            out=self.w2b[:, jj * c.D:(jj + 1) * c.D], in_=self.dsc["W2BFH"][j]),
            waits=[("ffmm2", em.c("ffmm2"))] + list(waits_extra),
            inc=f"w2l{j % 2}", dma=True)

    def _p6(self):
        self.soft_barrier()
        c, em, ab, af = self.cfg, self.em, self.ab, self.af
        yacc = af.alloc("yacc", c.NQC * c.D)
        self.vop(lambda v: v.memset(yacc[:, :], 0.0), inc="yve")
        hgrp = ab.alloc("hgrp", c.JGRP * c.Tq)
        w1b, w2b = self.w1b, self.w2b
        for grp in range(c.NGRP):
            for jj in range(c.JGRP):
                j = grp * c.JGRP + jj
                wb = w1b[:, (j % 4) * c.DCH * 128:][:, 0:c.DCH * 128]
                if j >= 4:
                    self._ffn_w1_load(j)
                if j >= c.JGRP:
                    self._ffn_w2_load(j)
                for th in range(c.NTH):
                    bank = self.ps[(j * c.NTH + th) % 4]
                    for dc in range(c.DCH):
                        wl = []
                        if dc == 0:
                            wl = [(f"w1l{j % 4}", 16 * (j // 4 + 1)),
                                  ("tokfr", em.c("tokfr")),
                                  ("hve", max(0, em.c("hve") - 3))]
                            if j == 0 and th == 0:
                                wl.append(("yve", em.c("yve")))
                        em.op("tensor", lambda t, wb=wb, dc=dc, th=th, bank=bank: t.matmul(
                            bank[:, 0:c.THW], wb[:, dc * 128:(dc + 1) * 128],
                            self.tokfm[:, dc * c.Tq + th * c.THW:][:, 0:c.THW],
                            start=(dc == 0), stop=(dc == c.DCH - 1)), waits=wl, inc="ffmm")
                    hv = hgrp[:, jj * c.Tq + th * c.THW:][:, 0:c.THW]
                    em.op("scalar", lambda s, hv=hv, bank=bank: s.activation(
                        hv, bank[:, 0:c.THW], AF.Gelu),
                        waits=[("ffmm", em.c("ffmm"))], inc="hve")
            for tl in range(c.NQC):
                for ds in range(c.NDS):
                    bank = self.ps[4 + (tl * c.NDS + ds) % 4]
                    for jj in range(c.JGRP):
                        wl = []
                        if jj == 0:
                            wl = [("hve", em.c("hve")), ("w2l0", em.c("w2l0")),
                                  ("w2l1", em.c("w2l1")),
                                  ("yve", max(0, em.c("yve") - 3))]
                        em.op("tensor", lambda t, jj=jj, tl=tl, ds=ds, bank=bank: t.matmul(
                            bank[:, 0:c.DSW],
                            hgrp[:, jj * c.Tq + tl * 128:][:, 0:128],
                            w2b[:, jj * c.D + ds * c.DSW:][:, 0:c.DSW],
                            start=(jj == 0), stop=(jj == c.JGRP - 1)), waits=wl, inc="ffmm2")
                    yv = yacc[:, tl * c.D + ds * c.DSW:][:, 0:c.DSW]
                    em.op("vector", lambda v, yv=yv, bank=bank: v.tensor_tensor(
                        yv, yv, bank[:, 0:c.DSW], AL.add),
                        waits=[("ffmm2", em.c("ffmm2"))], inc="yve")
        ab.freebuf("hgrp")
        ab.freebuf("w1b")
        ab.freebuf("w2b")
        ab.freebuf("tokfm")
        ybf = ab.alloc("ybf", 2 * c.D)
        for tl in range(c.NQC):
            yb = ybf[:, (tl % 2) * c.D:][:, 0:c.D]
            self.vop(lambda v, tl=tl, yb=yb: v.tensor_copy(yb, yacc[:, tl * c.D:(tl + 1) * c.D]),
                     waits=[("yve", em.c("yve"))] + (
                         [self.slw("ylw", tl - 2)] if tl >= 2 else []), inc="yve")
            self.dma_sl("ylw", tl, self.dsc["YLOC"][tl * 128:(tl + 1) * 128, :], yb,
                        waits=[("yve", em.c("yve"))])
        ab.freebuf("ybf")
        af.freebuf("yacc")
        tpc = c.AGCH // 128
        for cch in range(c.NAGC):
            rows = c.n_cores * c.AGCH
            em.op("gpsimd", lambda g, cch=cch, rows=rows: g.collective_compute(
                "AllGather", AL.bypass, replica_groups=[list(range(c.n_cores))],
                ins=[self.dsc["YLOC"][cch * c.AGCH:(cch + 1) * c.AGCH, :]],
                outs=[self.dsc["YALLC"][cch * rows:(cch + 1) * rows, :]]),
                waits=[self.slw("ylw", cch * tpc + i) for i in range(tpc)], inc="cc")

    def _p7(self):
        self.soft_barrier()
        c, em, ab, af = self.cfg, self.em, self.ab, self.af
        x1b = af.alloc("x1b", 2 * c.D)
        cg = ab.alloc("cgath", 4 * c.D)
        nsc = 2 * c.NQC
        srm = af.alloc("p7rm", 5 * nsc)
        yidx = self.ai.alloc("yidx", nsc)

        def R7(fn, waits=None):
            em.op("vector", fn, waits=waits if waits is not None else
                  [("ove", em.c("ove"))], inc="ove")
        sall = srm[:, 0:nsc]
        for k in range(c.NQC):
            for j in range(2):
                sv = self.myrec[:, 4 * k + 2 * j:4 * k + 2 * j + 1]
                R7(lambda v, sv=sv, k=k, j=j: v.tensor_scalar(
                    sall[:, 2 * k + j:2 * k + j + 1], sv, 0.0, None, AL.max),
                   waits=[("ind", em.c("ind")), ("ove", em.c("ove"))])
        R7(lambda v: v.tensor_scalar(sall, sall, float(c.N - 1), None, AL.min))
        self._ag_remap(srm[:, nsc:2 * nsc], sall, srm[:, 2 * nsc:5 * nsc],
                       yidx[:, :], R7)
        R7(lambda v: v.tensor_copy(yidx[:, :], srm[:, nsc:2 * nsc]))
        ove_remap = em.c("ove")
        ove_comb = {}
        for k in range(c.NQC):
            sl = k % 2
            xb = x1b[:, sl * c.D:][:, 0:c.D]
            cgv = cg[:, sl * 2 * c.D:][:, 0:2 * c.D]
            self.dma_sl("x1rl", k, xb, self.dsc["X1PARK"][k * 128:(k + 1) * 128, :],
                        waits=[(f"outw{k % 2}", 16 * (k // 2)) if k >= 2 else
                               ("x1pw", em.c("x1pw"))])
            gslot_guard = ove_comb[k - 2] if k >= 2 else ove_remap
            for j in range(2):
                em.op("gpsimd", lambda g, j=j, k=k, cgv=cgv: g.indirect_dma_start(
                    out=cgv[:, j * c.D:(j + 1) * c.D], out_offset=None,
                    in_=self.dsc["YALLC"],
                    in_offset=IndirectOffsetOnAxis(
                        ap=yidx[:, 2 * k + j:2 * k + j + 1].bitcast(U32), axis=0)),
                    waits=[("ove", gslot_guard), ("cc", em.c("cc"))],
                    inc=f"p7g{k % 2}", dma=True)
            q0 = self.myrec[:, 4 * k + 1:4 * k + 2]
            q1 = self.myrec[:, 4 * k + 3:4 * k + 4]
            self.vop(lambda v, q0=q0, xb=xb, cgv=cgv: v.scalar_tensor_tensor(
                xb, cgv[:, 0:c.D], q0, xb, AL.mult, AL.add),
                waits=[(f"p7g{k % 2}", 32 * (k // 2 + 1)), self.slw("x1rl", k)],
                inc="ove")
            self.vop(lambda v, q1=q1, xb=xb, cgv=cgv: v.scalar_tensor_tensor(
                xb, cgv[:, c.D:2 * c.D], q1, xb, AL.mult, AL.add),
                waits=[("ove", em.c("ove"))], inc="ove")
            ove_comb[k] = em.c("ove")
            self.dma(self.douts["out"][k * 128:(k + 1) * 128, :], xb,
                     waits=[("ove", em.c("ove"))], inc=f"outw{k % 2}")
        af.freebuf("x1b")
        af.freebuf("p7rm")
        ab.freebuf("cgath")

    def _taps(self):
        em = self.em
        allw = [(s, em.c(s)) for s in list(em.cnt.keys())
                if em.c(s) > 0 and not s.startswith("outw")]
        for t in self.taps:
            sname, shape, dt = self.tapspec[t]
            self.dma(self.douts["tap_" + t], self.dsc[sname], waits=allw, inc="outw")

    def _final(self):
        em = self.em
        waits = [(s, em.c(s)) for s in list(em.cnt.keys()) if em.c(s) > 0]
        em.op("sync", lambda s: None, waits=waits)

    def host_in_maps(self, inputs):
        c = self.cfg
        x = np.asarray(inputs["x"], np.float32).reshape(c.B, c.T, c.D)
        wqT = np.ascontiguousarray(np.asarray(inputs["wq"], np.float32).T)
        wkT = np.ascontiguousarray(np.asarray(inputs["wk"], np.float32).T)
        wvT = np.ascontiguousarray(np.asarray(inputs["wv"], np.float32).T)
        woT = np.ascontiguousarray(np.asarray(inputs["wo"], np.float32).T)
        gq = np.asarray(inputs["gamma_q"], np.float32)
        gk = np.asarray(inputs["gamma_k"], np.float32)
        Wg = np.ascontiguousarray(np.asarray(inputs["Wg"], np.float32))
        bg = np.asarray(inputs["bg"], np.float32)
        W1 = np.asarray(inputs["W1"], np.float32)
        W2 = np.asarray(inputs["W2"], np.float32)
        gqrep = np.tile(gq, 2)[None, :].astype(np.float32)
        gkrep = np.tile(gk, 2)[None, :].astype(np.float32)
        bgrep = np.ascontiguousarray(np.broadcast_to(bg[None, :], (128, c.E)).astype(np.float32))
        maps = []
        for r in range(c.n_cores):
            b, half = r // 2, r % 2
            t0 = half * c.Tq
            lo = t0 - c.Tq
            slab = np.zeros((c.S, c.D), np.float32)
            src_lo = max(lo, 0)
            slab[src_lo - lo:] = x[b, src_lo:t0 + c.Tq]
            kvalid = np.zeros((128, c.NSC), np.float32)
            for scb in range(c.NSC):
                rows = lo + scb * 128 + np.arange(128)
                kvalid[:, scb] = (rows >= 0).astype(np.float32)
            e = r // 2
            W1H = np.ascontiguousarray(
                W1[e].reshape(c.DCH, 128, c.NH1, c.PCW).transpose(2, 0, 1, 3).reshape(
                    c.NW1U, 128, c.PCW))
            W2H = np.ascontiguousarray(W2[e].reshape(c.NJ, 128, c.D))
            maps.append(dict(
                x_slab=slab, wqT=wqT, wkT=wkT, wvT=wvT, woT=woT,
                gqrep=gqrep, gkrep=gkrep, Wg=Wg, bgrep=bgrep,
                W1H=W1H, W2H=W2H,
                t0col=np.full((128, 1), float(lo), np.float32),
                rankcol=np.full((128, 1), float(r), np.float32),
                hcol=np.full((128, 1), float(half), np.float32),
                ecol=np.full((128, 1), float(e), np.float32),
                kvalid=kvalid,
            ))
        return maps

    def assemble(self, results):
        c = self.cfg
        return np.concatenate([np.asarray(r["out"]) for r in results], axis=0).reshape(
            c.B, c.T, c.D)


def kernel(L0_attn_x, L0_attn_wq, L0_attn_wk, L0_attn_wv, L0_attn_wo,
           L0_attn_gamma_q, L0_attn_gamma_k, L0_ffn_Wg, L0_ffn_bg,
           L0_ffn_W1, L0_ffn_W2):
    cfg = Cfg.full()
    mk = MK(cfg)
    nc = mk.build()
    maps = mk.host_in_maps(dict(
        x=L0_attn_x, wq=L0_attn_wq, wk=L0_attn_wk, wv=L0_attn_wv, wo=L0_attn_wo,
        gamma_q=L0_attn_gamma_q, gamma_k=L0_attn_gamma_k,
        Wg=L0_ffn_Wg, bg=L0_ffn_bg, W1=L0_ffn_W1, W2=L0_ffn_W2))
    res = run_bass_kernel_spmd(nc, maps, core_ids=list(range(cfg.n_cores)))
    out = mk.assemble(res.results)
    return np.ascontiguousarray(out.astype(np.float32))


# version 13
# speedup vs baseline: 1.0186x; 1.0186x over previous
import math
import contextlib
import numpy as np
import ml_dtypes

import concourse.bass as bass
import concourse.mybir as mybir
from concourse.alu_op_type import AluOpType
from concourse.bass import IndirectOffsetOnAxis
from concourse.bass_utils import run_bass_kernel_spmd

F32 = mybir.dt.float32
BF16 = mybir.dt.bfloat16
U32 = mybir.dt.uint32
I32 = mybir.dt.int32
AF = mybir.ActivationFunctionType
AL = AluOpType
bf16 = ml_dtypes.bfloat16
AX = mybir.AxisListType.X

ALL_PHASES = ("p0", "p1", "p2", "p3", "p4", "p5", "p5b", "p6", "p7")


class Cfg:
    def __init__(self, B=4, T=2048, D=2048, H=16, HK=4, Dh=128, WIN=1024,
                 E=4, HID=8192, n_cores=8, n_search=32):
        self.B, self.T, self.D, self.H, self.HK, self.Dh = B, T, D, H, HK, Dh
        self.WIN, self.E, self.HID, self.n_cores = WIN, E, HID, n_cores
        self.n_search = n_search
        self.N = B * T
        self.G = H // HK
        self.Tq = self.N // n_cores
        self.S = 2 * self.Tq
        self.CAP = self.N // E
        assert self.CAP // 2 == self.Tq
        self.DCH = D // 128
        self.NQC = self.Tq // 128
        self.NSC = self.S // 128
        self.KVW = HK * Dh
        self.QW = H * Dh
        self.ESL = min(512, self.QW)
        self.NES = self.QW // self.ESL
        self.ICW = min(512, self.Tq)
        self.NIC = self.Tq // self.ICW
        self.NJ = HID // 128
        self.JGRP = min(8, self.NJ)
        self.NGRP = self.NJ // self.JGRP
        self.NCH = self.N // 128
        self.LSLOT = self.CAP + 256
        assert self.LSLOT % 128 == 0
        self.WTAB = 2 * Dh
        assert self.ESL % self.WTAB == 0 and self.KVW % self.WTAB == 0
        self.THW = min(512, self.Tq)
        self.NTH = self.Tq // self.THW
        self.DSW = min(512, D)
        self.NDS = D // self.DSW
        self.AGCH = min(256, self.Tq)
        self.NAGC = self.Tq // self.AGCH
        self.PCW = min(2048, HID)
        self.NH1 = HID // self.PCW
        self.NW1U = self.DCH * self.NH1
        self.JPH = self.PCW // 128

    @classmethod
    def full(cls):
        return cls()

    @classmethod
    def small(cls):
        return cls(B=4, T=256, D=256, H=4, HK=2, Dh=128, WIN=128, E=4, HID=512)


class Emitter:
    ENGINES = ("sync", "vector", "scalar", "gpsimd", "tensor")

    def __init__(self, nc, stack):
        self.nc = nc
        self.stack = stack
        self.ops = {e: [] for e in self.ENGINES}
        self.sems = {}
        self.cnt = {}

    def sem(self, name):
        if name not in self.sems:
            self.sems[name] = self.stack.enter_context(self.nc.semaphore(name))
            self.cnt[name] = 0
        return self.sems[name]

    def op(self, engine, fn, waits=(), inc=None, dma=False):
        wl = []
        for w in waits:
            if isinstance(w, tuple):
                name, v = w
            else:
                name, v = w, self.cnt.get(w, 0)
            self.sem(name)
            if v > 0:
                wl.append((name, v))
        amt = 0
        if inc is not None:
            self.sem(inc)
            amt = 16 if dma else 1
            self.cnt[inc] = self.cnt.get(inc, 0) + amt
        self.ops[engine].append((fn, wl, inc, amt))

    def c(self, name):
        return self.cnt.get(name, 0)

    def materialize(self, block):
        for ename in self.ENGINES:
            mine = self.ops[ename]
            if not mine:
                continue
            deco = getattr(block, ename)

            def make(mine=mine):
                def body(eng):
                    for fn, wl, inc, amt in mine:
                        for sname, v in wl:
                            eng.wait_ge(self.sems[sname], v)
                        inst = fn(eng)
                        if inc is not None:
                            if inst is not None:
                                inst.then_inc(self.sems[inc], amt)
                            else:
                                eng.sem_inc(self.sems[inc], amt)
                return body
            deco(make())


class Arena:
    def __init__(self, nc, stack, name, cols, dt):
        self.t = stack.enter_context(nc.sbuf_tensor(name, [128, cols], dt))
        self.free_ranges = [(0, cols)]
        self.named = {}
        self.name = name
        self.hwm = 0

    def alloc(self, name, n, high=False):
        assert name not in self.named, name
        nr = (n + 15) // 16 * 16
        if high:
            for i in range(len(self.free_ranges) - 1, -1, -1):
                a, b = self.free_ranges[i]
                if b - a >= nr:
                    st = b - nr
                    if st == a:
                        self.free_ranges.pop(i)
                    else:
                        self.free_ranges[i] = (a, st)
                    self.named[name] = (st, nr)
                    self.hwm = max(self.hwm, max(a2 + n2 for a2, n2 in self.named.values()))
                    return self.t[:, st:st + n]
            raise RuntimeError(f"arena {self.name} OOM: {name} needs {n}, free={self.free_ranges}")
        for i, (a, b) in enumerate(self.free_ranges):
            if b - a >= nr:
                if a + nr == b:
                    self.free_ranges.pop(i)
                else:
                    self.free_ranges[i] = (a + nr, b)
                self.named[name] = (a, nr)
                self.hwm = max(self.hwm, max(a2 + n2 for a2, n2 in self.named.values()))
                return self.t[:, a:a + n]
        raise RuntimeError(f"arena {self.name} OOM: {name} needs {n}, free={self.free_ranges}")

    def freebuf(self, name):
        a, n = self.named.pop(name)
        self.free_ranges.append((a, a + n))
        self.free_ranges.sort()
        merged = []
        for a, b in self.free_ranges:
            if merged and merged[-1][1] == a:
                merged[-1][1] = b
            else:
                merged.append([a, b])
        self.free_ranges = [tuple(x) for x in merged]

    def view(self, name):
        a, n = self.named[name]
        return self.t[:, a:a + n]


class MK:
    def __init__(self, cfg: Cfg, taps=(), phases=ALL_PHASES):
        self.cfg = cfg
        self.taps = list(taps)
        self.phases = phases

    def build(self):
        cfg = self.cfg
        nc = bass.Bass("TRN2", target_bir_lowering=False)
        self.nc = nc
        c = cfg
        din = {}

        def inp(name, shape, dt=F32):
            din[name] = nc.dram_tensor(name, list(shape), dt, kind="ExternalInput").ap()

        inp("x_slab", [c.S, c.D])
        inp("wqT", [c.D, c.QW])
        inp("wkT", [c.D, c.KVW])
        inp("wvT", [c.D, c.KVW])
        inp("woT", [c.QW, c.D])
        inp("gqrep", [1, c.WTAB])
        inp("gkrep", [1, c.WTAB])
        inp("Wg", [c.D, c.E])
        inp("bgrep", [128, c.E])
        inp("W1H", [c.NW1U, 128, c.PCW])
        inp("W2H", [c.NJ, 128, c.D])
        inp("t0col", [128, 1])
        inp("rankcol", [128, 1])
        inp("hcol", [128, 1])
        inp("ecol", [128, 1])
        inp("kvalid", [128, c.NSC])
        self.din = din

        dsc = {}

        def scr(name, shape, dt, addr_space="Local"):
            dsc[name] = nc.dram_tensor(name, list(shape), dt, addr_space=addr_space).ap()

        scr("XT1", [c.S, c.D], BF16)
        scr("XQ", [c.Tq, c.QW], BF16)
        scr("XK", [c.S, c.KVW], BF16)
        scr("XT2loc", [c.Tq, c.D], BF16)
        scr("XT2allC", [c.N, c.D], BF16, addr_space="Shared")
        scr("LGloc", [c.Tq, c.E], F32)
        scr("LGall", [c.N, c.E], F32, addr_space="Shared")
        scr("RECd", [128 * c.NCH, 4], F32)
        scr("LISTloc", [c.E * c.LSLOT, 4], F32)
        scr("LISTall", [c.n_cores * c.E * c.LSLOT, 4], F32, addr_space="Shared")
        scr("LISTcmb", [c.E * c.LSLOT, 4], F32)
        scr("XTOK", [c.Tq, c.D], BF16)
        scr("YLOC", [c.Tq, c.D], BF16)
        scr("YALLC", [c.N, c.D], BF16, addr_space="Shared")
        scr("X1PARK", [c.Tq, c.D], F32)
        scr("X2FMDBG", [c.D, c.Tq], BF16)
        scr("W1BFH", [c.NW1U, 128, c.PCW], BF16)
        scr("W2BFH", [c.NJ, 128, c.D], BF16)
        self.dsc = dsc

        douts = {}
        douts["out"] = nc.dram_tensor("out", [c.Tq, c.D], F32, kind="ExternalOutput").ap()
        self.tapspec = {
            "xt1": ("XT1", [c.S, c.D], BF16),
            "xq": ("XQ", [c.Tq, c.QW], BF16),
            "xk": ("XK", [c.S, c.KVW], BF16),
            "x1": ("X1PARK", [c.Tq, c.D], F32),
            "xt2": ("XT2loc", [c.Tq, c.D], BF16),
            "lgall": ("LGall", [c.N, c.E], F32),
            "lgloc": ("LGloc", [c.Tq, c.E], F32),
            "x2fmdbg": ("X2FMDBG", [c.D, c.Tq], BF16),
            "rec": ("RECd", [128 * c.NCH, 4], F32),
            "listcmb": ("LISTcmb", [c.E * c.LSLOT, 4], F32),
            "xtok": ("XTOK", [c.Tq, c.D], BF16),
            "yloc": ("YLOC", [c.Tq, c.D], BF16),
            "yall": ("YALLC", [c.N, c.D], BF16),
        }
        for t in self.taps:
            sname, shape, dt = self.tapspec[t]
            douts["tap_" + t] = nc.dram_tensor("tap_" + t, list(shape), dt,
                                               kind="ExternalOutput").ap()
        self.douts = douts

        with contextlib.ExitStack() as stack:
            em = self.em = Emitter(nc, stack)
            self.af = Arena(nc, stack, "AF32", 25000, F32)
            self.ab = Arena(nc, stack, "ABF", 50000, BF16)
            self.ai = Arena(nc, stack, "AI32", max(600, self.cfg.WTAB + 64), I32)
            self.ps = [stack.enter_context(nc.psum_tensor(f"psb{i}", [128, 512], F32))
                       for i in range(8)]
            for ph in self.phases:
                getattr(self, "_" + ph)()
            self._taps()
            self._final()
            block = stack.enter_context(nc.Block())
            em.materialize(block)
        nc.finalize()
        return nc

    def dma(self, out, in_, waits=(), inc="ld"):
        self.em.op("sync", lambda s, out=out, in_=in_: s.dma_start(out=out, in_=in_),
                   waits=waits, inc=inc, dma=True)

    def dma_t(self, out, in_, waits=(), inc="ld"):
        self.em.op("sync", lambda s, out=out, in_=in_: s.dma_start(out=out, in_=in_,
                                                                   transpose=True),
                   waits=waits, inc=inc, dma=True)

    def vop(self, fn, waits=(), inc="ve"):
        self.em.op("vector", fn, waits=waits, inc=inc)

    def iota_f32(self, dst_f32, pattern, base, cm, waits=()):
        it = self.ai.view("iotatmp")
        npart = dst_f32.shape[0]
        n = dst_f32.shape[1]
        self.em.op("gpsimd", lambda g: g.iota(it[0:npart, 0:n], pattern=pattern, base=base,
                                              channel_multiplier=cm),
                   waits=list(waits) + [("iocast", self.em.c("iocast"))], inc="gp")
        self.em.op("vector", lambda v: v.tensor_copy(dst_f32, it[0:npart, 0:n]),
                   waits=[("gp", self.em.c("gp"))], inc="iocast")

    def f2i(self, dst_i32, src_f32, waits=(), inc="ve"):
        self.em.op("vector", lambda v: v.tensor_copy(dst_i32, src_f32), waits=waits, inc=inc)

    def dma_sl(self, stream, k, out, in_, waits=(), transpose=False):
        sem = f"{stream}{k % 2}"
        fn = (lambda s, out=out, in_=in_: s.dma_start(out=out, in_=in_, transpose=transpose))
        self.em.op("sync", fn, waits=waits, inc=sem, dma=True)

    def slw(self, stream, k):
        return (f"{stream}{k % 2}", 16 * (k // 2 + 1))

    def sl_full(self, stream):
        return [(f"{stream}0", self.em.c(f"{stream}0")),
                (f"{stream}1", self.em.c(f"{stream}1"))]

    def load_cast_cols(self, dst_bf, dram_cols_ap, w, nch=None, extra_guard=(), ngrp=4):
        em = self.em
        c = self.cfg
        if nch is None:
            nch = c.DCH
        ngrp = min(ngrp, nch)
        if not hasattr(self, "_wk"):
            self._wk = 0
        per = nch // ngrp
        for g in range(ngrp):
            k = self._wk
            slot = k % 2
            wl, wc = f"wl{slot}", f"wc{slot}"
            stg = self.wstg[:, slot * self.wstg_w:][:, 0:per * w]
            self.em.op("scalar", lambda s, g=g, stg=stg: s.dma_start(
                out=stg.rearrange("p (c j) -> p c j", c=per),
                in_=dram_cols_ap[g * per * 128:(g + 1) * per * 128, :].rearrange(
                    "(c p) j -> p c j", p=128)),
                waits=[(wc, em.c(wc))], inc=wl, dma=True)
            self.vop(lambda v, g=g, stg=stg: v.tensor_copy(
                dst_bf[:, g * per * w:(g + 1) * per * w], stg),
                waits=[(wl, em.c(wl))] + list(extra_guard), inc=wc)
            self._wk += 1

    def soft_barrier(self, exclude=()):
        em = self.em
        waits = [(s, em.c(s)) for s in list(em.cnt.keys())
                 if em.c(s) > 0 and s not in exclude]
        for eng in ("sync", "vector", "scalar", "gpsimd", "tensor"):
            em.op(eng, lambda e: None, waits=waits)

    def load_cast(self, dst_bf, dram_rows_ap, nch, w, extra_guard=()):
        em = self.em
        stg = self.wstg
        if not hasattr(self, "_wk"):
            self._wk = 0
        for dc in range(nch):
            k = self._wk
            slot = k % 2
            wl, wc = f"wl{slot}", f"wc{slot}"
            sv = stg[:, slot * self.wstg_w:][:, 0:w]
            self.em.op("scalar", lambda s, sv=sv, dc=dc: s.dma_start(
                out=sv, in_=dram_rows_ap[dc * 128:(dc + 1) * 128, :]),
                waits=[(wc, em.c(wc))], inc=wl, dma=True)
            self.vop(lambda v, sv=sv, dc=dc: v.tensor_copy(dst_bf[:, dc * w:(dc + 1) * w], sv),
                     waits=[(wl, em.c(wl))] + list(extra_guard), inc=wc)
            self._wk += 1

    def _floor_div(self, ef, srcf, q, scrf, scri, R):
        u = scrf
        R(lambda v: v.tensor_scalar(u, srcf, 1.0 / q, None, AL.mult))
        R(lambda v: v.tensor_copy(scri, u))
        R(lambda v: v.tensor_copy(ef, scri))
        R(lambda v: v.tensor_tensor(u, u, ef, AL.subtract))
        R(lambda v: v.tensor_scalar(u, u, 0.0, None, AL.is_lt))
        R(lambda v: v.tensor_tensor(ef, ef, u, AL.subtract))

    def _ag_remap(self, dstf, srcf, scrf3, scri, R):
        c = self.cfg
        w = srcf.shape[1]
        e1 = scrf3[:, 0:w]
        e2 = scrf3[:, w:2 * w]
        m1 = scrf3[:, 2 * w:3 * w]
        self._floor_div(e1, srcf, float(c.Tq), m1, scri, R)
        R(lambda v: v.scalar_tensor_tensor(m1, e1, -float(c.Tq), srcf, AL.mult, AL.add))
        self._floor_div(e2, m1, float(c.AGCH), dstf, scri, R)
        R(lambda v: v.scalar_tensor_tensor(
            dstf, e2, float((c.n_cores - 1) * c.AGCH), m1, AL.mult, AL.add))
        R(lambda v: v.scalar_tensor_tensor(dstf, e1, float(c.AGCH), dstf, AL.mult, AL.add))

    def _pc_in(self, k, q):
        src, dst, u, w = self.pc_units[k]
        s = k % 2
        waits = [(f"pcc{s}", k // 2)] if k >= 2 else []
        self.em.op(q, lambda e, u=u, s=s, w=w, src=src: e.dma_start(
            out=self.pcf[:, s * self.cfg.PCW:][:, 0:w], in_=self.din[src][u]),
            waits=waits, inc=f"pcl{s}", dma=True)

    def pc_step(self, n=1, q="scalar"):
        em = self.em
        for _ in range(n):
            k = self._pc_next
            if k >= len(self.pc_units):
                return
            while self._pc_next_in <= k + 1 and self._pc_next_in < len(self.pc_units):
                self._pc_in(self._pc_next_in, q)
                self._pc_next_in += 1
            src, dst, u, w = self.pc_units[k]
            s = k % 2
            waits = [(f"pcl{s}", 16 * (k // 2 + 1))]
            if k >= 2:
                waits.append((f"pcw{s}", 16 * (k // 2)))
            em.op("scalar", lambda e, s=s, w=w: e.activation(
                self.pcb[:, s * self.cfg.PCW:][:, 0:w],
                self.pcf[:, s * self.cfg.PCW:][:, 0:w], AF.Copy),
                waits=waits, inc=f"pcc{s}")
            em.op("sync", lambda e, u=u, s=s, w=w, dst=dst: e.dma_start(
                out=self.dsc[dst][u], in_=self.pcb[:, s * self.cfg.PCW:][:, 0:w]),
                waits=[(f"pcc{s}", k // 2 + 1)], inc=f"pcw{s}", dma=True)
            self._pc_next = k + 1

    def _p0(self):
        c, em, af, ab = self.cfg, self.em, self.af, self.ab
        self.ai.alloc("iotatmp", self.cfg.WTAB)
        self.ones_bf = ab.alloc("ones_bf", 2)
        self.onesr_f = af.alloc("onesr_f", 128)
        self.halfpi = af.alloc("halfpi", 1)
        self.negBpad = af.alloc("negBpad", max(c.NSC, 2))
        self.epsln = af.alloc("epsln", 1)
        self.gq_bc = af.alloc("gq_bc", c.WTAB)
        self.gk_bc = af.alloc("gk_bc", c.WTAB)
        self.misc = af.alloc("misc", 1088)
        self.ones_f = af.alloc("ones_f", 1)
        self.wstg_w = max(c.D, c.DSW, 512)
        self.wstg = af.alloc("wstg", 2 * self.wstg_w)
        self.lnst = af.alloc("lnst", 32)
        self.myrec = af.alloc("myrec", c.NQC * 4)
        self.costab = af.alloc("costab", c.NSC * c.WTAB)
        self.sintab = af.alloc("sintab", c.NSC * c.WTAB)
        self.tri_bf = ab.alloc("tri_bf", 128)

        self.vop(lambda v: v.memset(self.ones_bf[:, :], 1.0))
        self.vop(lambda v: v.memset(self.ones_f[:, :], 1.0))
        self.vop(lambda v: v.memset(self.onesr_f[0:1, :], 1.0))
        self.vop(lambda v: v.memset(self.halfpi[:, :], math.pi / 2))
        self.vop(lambda v: v.memset(self.epsln[:, :], 1e-5))
        self.vop(lambda v: v.memset(self.tri_bf[:, :], 1.0))
        em.op("gpsimd", lambda g: g.affine_select(
            self.tri_bf[:, :], self.tri_bf[:, :], pattern=[[1, 128]],
            compare_op=AL.is_ge, fill=0.0, base=-1, channel_multiplier=-1),
            waits=["ve"], inc="gp")

        invf = af.alloc("invf", c.WTAB)
        self.iota_f32(invf[0:1, :], [[0, 4], [1, c.Dh // 2]], 0, 0)
        em.op("scalar", lambda s: s.activation(invf[0:1, :], invf[0:1, :], AF.Exp,
                                               scale=-2.0 * math.log(10000.0) / c.Dh),
              waits=["iocast"], inc="ac")
        invf_bc = af.alloc("invf_bc", c.WTAB)
        em.op("tensor", lambda t: t.matmul(self.ps[6][:, 0:c.WTAB], self.onesr_f[0:1, :],
                                           invf[0:1, :], start=True, stop=True),
              waits=["ac", "ve"], inc="mm")
        self.vop(lambda v: v.tensor_copy(invf_bc[:, :], self.ps[6][:, 0:c.WTAB]), waits=["mm"])

        gqr = af.alloc("gqr", c.WTAB)
        gkr = af.alloc("gkr", c.WTAB)
        self.dma(gqr[0:1, :], self.din["gqrep"], inc="ldgq")
        self.dma(gkr[0:1, :], self.din["gkrep"], inc="ldgk")
        em.op("tensor", lambda t: t.matmul(self.ps[6][:, 0:c.WTAB], self.onesr_f[0:1, :],
                                           gqr[0:1, :], start=True, stop=True),
              waits=["ldgq", "ve"], inc="mm")
        self.vop(lambda v: v.tensor_scalar(self.gq_bc[:, :], self.ps[6][:, 0:c.WTAB],
                                           1.0 / math.sqrt(c.Dh), None, AL.mult), waits=["mm"])
        em.op("tensor", lambda t: t.matmul(self.ps[6][:, 0:c.WTAB], self.onesr_f[0:1, :],
                                           gkr[0:1, :], start=True, stop=True),
              waits=["ve", "ldgk"], inc="mm")
        self.vop(lambda v: v.tensor_copy(self.gk_bc[:, :], self.ps[6][:, 0:c.WTAB]), waits=["mm"])

        bb = self.misc[:, 0:c.WTAB]
        self.vop(lambda v: v.tensor_tensor(bb[0:1, :], gqr[0:1, :], gkr[0:1, :], AL.mult),
                 waits=["ldgq", "ldgk"])
        em.op("scalar", lambda s: s.activation(bb[0:1, :], bb[0:1, :], AF.Abs),
              waits=["ve"], inc="ac")
        bmax = self.misc[:, c.WTAB:c.WTAB + 1]
        self.vop(lambda v: v.reduce_max(bmax[0:1, :], bb[0:1, 0:c.Dh], axis=AX), waits=["ac"])
        self.vop(lambda v: v.tensor_scalar(bmax[0:1, :], bmax[0:1, :],
                                           -1.0 / math.sqrt(c.Dh), None, AL.mult), waits=["ve"])
        em.op("tensor", lambda t: t.matmul(self.ps[6][:, 0:1], self.onesr_f[0:1, :],
                                           bmax[0:1, :], start=True, stop=True),
              waits=["ve"], inc="mm")
        negB = af.alloc("negB", 1)
        self.vop(lambda v: v.tensor_copy(negB[:, :], self.ps[6][:, 0:1]), waits=["mm"])
        kv_in = af.alloc("kv_in", c.NSC)
        self.dma(kv_in[:, :], self.din["kvalid"], inc="ldkv")
        self.vop(lambda v: v.tensor_scalar(self.negBpad[:, 0:c.NSC], kv_in[:, :],
                                           -1.0, 1e30, AL.add, AL.mult), waits=["ldkv", "ve"])
        self.vop(lambda v: v.tensor_scalar(self.negBpad[:, 0:c.NSC], self.negBpad[:, 0:c.NSC],
                                           negB[:, 0:1], None, AL.add), waits=["ve"])

        t0c = af.alloc("t0c", 1)
        self.dma(t0c[:, :], self.din["t0col"], inc="ldt0")
        scol = self.misc[:, 256:257]
        u = self.misc[:, 260:260 + c.WTAB]
        kfi = af.alloc("kfi", c.WTAB)
        kff = self.misc[:, 260 + c.WTAB:260 + 2 * c.WTAB]
        for sc in range(c.NSC):
            cosv = self.costab[:, sc * c.WTAB:(sc + 1) * c.WTAB]
            sinv = self.sintab[:, sc * c.WTAB:(sc + 1) * c.WTAB]
            self.iota_f32(scol, [[0, 1]], sc * 128, 1, waits=["ve"])
            self.vop(lambda v: v.tensor_tensor(scol, scol, t0c[:, :], AL.add),
                     waits=["iocast", "ldt0"])
            self.vop(lambda v, cosv=cosv: v.tensor_scalar(cosv, invf_bc[:, :], scol[:, 0:1],
                                                          None, AL.mult), waits=["ve"])
            def range_red(dst, bias025, wfirst):
                self.vop(lambda v: v.tensor_scalar(u, dst, 1.0 / (2 * math.pi),
                                                   bias025, AL.mult, AL.add), waits=wfirst)
                self.f2i(kfi.bitcast(I32), u, waits=["ve"])
                self.vop(lambda v: v.tensor_copy(kff, kfi.bitcast(I32)), waits=["ve"])
                self.vop(lambda v: v.scalar_tensor_tensor(dst, kff, -1.0, u,
                                                          AL.mult, AL.add), waits=["ve"])
                self.vop(lambda v: v.tensor_scalar(kff, dst, 0.5, None, AL.is_gt),
                         waits=["ve"])
                self.vop(lambda v: v.tensor_tensor(dst, dst, kff, AL.subtract), waits=["ve"])
                self.vop(lambda v: v.tensor_scalar(kff, dst, -0.5, None, AL.is_lt),
                         waits=["ve"])
                self.vop(lambda v: v.tensor_tensor(dst, dst, kff, AL.add), waits=["ve"])
            self.vop(lambda v, cosv=cosv, sinv=sinv: v.tensor_copy(sinv, cosv), waits=["ve"])
            range_red(sinv, 0.0, ["ve"])
            em.op("scalar", lambda s, sinv=sinv: s.activation(sinv, sinv, AF.Sin,
                                                              scale=2 * math.pi),
                  waits=["ve"], inc="ac")
            for h2 in range(2):
                self.vop(lambda v, sinv=sinv, h2=h2: v.tensor_scalar(
                    sinv[:, h2 * c.Dh:h2 * c.Dh + c.Dh // 2],
                    sinv[:, h2 * c.Dh:h2 * c.Dh + c.Dh // 2], -1.0, None, AL.mult),
                    waits=["ac"])
            range_red(cosv, 0.25, ["ac"])
            em.op("scalar", lambda s, cosv=cosv: s.activation(
                cosv, cosv, AF.Sin, scale=2 * math.pi), waits=["ve"], inc="ac")
        af.freebuf("invf")
        af.freebuf("invf_bc")
        af.freebuf("gqr")
        af.freebuf("gkr")
        af.freebuf("kv_in")
        af.freebuf("kfi")
        af.freebuf("t0c")
        af.freebuf("negB")
        self.pcf = af.alloc("pcstgf", 2 * c.PCW, high=True)
        self.pcb = ab.alloc("pcstgb", 2 * c.PCW, high=True)
        self.pc_units = ([("W1H", "W1BFH", u, c.PCW) for u in range(c.NW1U)] +
                         [("W2H", "W2BFH", u, c.D) for u in range(c.NJ)])
        self._pc_next_in = 0
        self._pc_next = 0

    def _jblocks(self, ic):
        c = self.cfg
        i0 = ic * c.ICW
        jlo = max(0, (c.Tq + i0 - c.WIN) // 128)
        jhi = min(c.NSC - 1, (c.Tq + i0 + c.ICW - 1) // 128)
        return [jb for jb in range(jlo, jhi + 1)
                if self._mask_kind(c.Tq + i0 - jb * 128) != "empty"]

    def _mask_kind(self, c0):
        c = self.cfg
        dmin, dmax = c0 - 127, c0 + c.ICW - 1
        if dmax < 0 or dmin > c.WIN:
            return "empty"
        if dmin >= 0 and dmax <= c.WIN:
            return "full"
        return "partial"

    def _p1(self):
        self.soft_barrier()
        c, em, af, ab = self.cfg, self.em, self.af, self.ab
        xstage = af.alloc("xstage", 2 * c.D)
        xlnbf = ab.alloc("xlnbf", 2 * c.D)
        st = self.lnst
        nbn = (c.D + 511) // 512
        for sc in range(c.NSC):
            xt = xstage[:, (sc % 2) * c.D:][:, 0:c.D]
            ob = xlnbf[:, (sc % 2) * c.D:][:, 0:c.D]
            self.dma_sl("xld", sc, xt, self.din["x_slab"][sc * 128:(sc + 1) * 128, :],
                        waits=[("ve", max(0, em.c("ve") - 2))] if sc >= 2 else [])
            for i in range(nbn):
                w = min(512, c.D - i * 512)
                self.vop(lambda v, xt=xt, i=i, w=w: v.bn_stats(
                    st[:, 6 * i:6 * i + 6], xt[:, i * 512:i * 512 + w]),
                    waits=[self.slw("xld", sc)])
            self.vop(lambda v: v.bn_aggr(st[:, 24:26], st[:, 0:6 * nbn]), waits=["ve"])
            em.op("scalar", lambda s: s.activation(st[:, 26:27], st[:, 25:26], AF.Sqrt,
                                                   bias=self.epsln[:, :]), waits=["ve"], inc="ac")
            self.vop(lambda v: v.reciprocal(st[:, 26:27], st[:, 26:27]), waits=["ac"])
            self.vop(lambda v: v.tensor_scalar(st[:, 27:28], st[:, 24:25], -1.0, None, AL.mult),
                     waits=["ve"])
            self.vop(lambda v, xt=xt, ob=ob: v.tensor_scalar(
                ob, xt, st[:, 27:28], st[:, 26:27], AL.add, AL.mult),
                waits=["ve"] + ([self.slw("xt1w", sc - 2)] if sc >= 2 else []))
            self.dma_sl("xt1w", sc, self.dsc["XT1"][sc * 128:(sc + 1) * 128, :], ob,
                        waits=[("ve", em.c("ve"))])
            self.pc_step(1)
        af.freebuf("xstage")
        ab.freebuf("xlnbf")

    def _p2(self):
        self.soft_barrier()
        c, em, af, ab = self.cfg, self.em, self.af, self.ab
        xfm = ab.alloc("xfm", c.DCH * c.Tq)
        wkv = ab.alloc("wkv", 2 * c.DCH * c.KVW)
        self.load_cast_cols(wkv[:, 0:c.DCH * c.KVW], self.din["wkT"], c.KVW)
        self.load_cast_cols(wkv[:, c.DCH * c.KVW:], self.din["wvT"], c.KVW)
        self.vtm = ab.alloc("vtm", c.NSC * c.KVW)
        af.alloc("ropebuf", c.ESL + 64)
        kstage = ab.alloc("kstage", 2 * c.KVW)
        for hf in range(2):
            nw = (hf + 1) * c.NQC
            hw = [("xt1w0", 16 * ((nw + 1) // 2)), ("xt1w1", 16 * (nw // 2))]
            for dc in range(c.DCH):
                self.dma_t(xfm[:, dc * c.Tq:(dc + 1) * c.Tq],
                           self.dsc["XT1"][hf * c.Tq:(hf + 1) * c.Tq, dc * 128:(dc + 1) * 128],
                           waits=hw + [("mm", em.c("mm"))],
                           inc="xfmr")
            for tcl in range(c.NQC):
                tc = hf * c.NQC + tcl
                for wi in range(2):
                    bank = self.ps[4 + (tc * 2 + wi) % 2]
                    for dc in range(c.DCH):
                        wl = []
                        if dc == 0:
                            wl = [("xfmr", em.c("xfmr")), ("wc0", em.c("wc0")),
                                  ("wc1", em.c("wc1")), ("bkfree", max(0, em.c("bkfree") - 1))] + (
                                  [self.slw("xkw", tc - 2)] if tc >= 2 else [])
                        em.op("tensor", lambda t, tcl=tcl, wi=wi, dc=dc, bank=bank: t.matmul(
                            bank[:, 0:c.KVW],
                            xfm[:, dc * c.Tq + tcl * 128:][:, 0:128],
                            wkv[:, wi * c.DCH * c.KVW + dc * c.KVW:][:, 0:c.KVW],
                            start=(dc == 0), stop=(dc == c.DCH - 1)), waits=wl, inc="mm")
                    if wi == 1:
                        self.vop(lambda v, tc=tc, bank=bank: v.tensor_copy(
                            self.vtm[:, tc * c.KVW:(tc + 1) * c.KVW], bank[:, 0:c.KVW]),
                            waits=[("mm", em.c("mm"))])
                        self.em.op("vector", lambda v: None, waits=["ve"], inc="bkfree")
                    else:
                        ko = kstage[:, (tc % 2) * c.KVW:][:, 0:c.KVW]
                        self._qk_epilogue(bank, ko, tc, is_q=False)
                        self.dma_sl("xkw", tc, self.dsc["XK"][tc * 128:(tc + 1) * 128, :], ko,
                                    waits=[("epid", em.c("epid"))])
                    self.pc_step(1)
        ab.freebuf("wkv")
        wq = ab.alloc("wq", 2 * c.DCH * c.ESL)
        qstage = ab.alloc("qstage", 2 * c.ESL)
        mm_after_kv = em.c("mm")
        mm_after_es = []
        for es in range(c.NES):
            wqv = wq[:, (es % 2) * c.DCH * c.ESL:][:, 0:c.DCH * c.ESL]
            guard = [("mm", mm_after_es[es - 2])] if es >= 2 else [("mm", mm_after_kv)]
            self.load_cast_cols(wqv, self.din["wqT"][:, es * c.ESL:(es + 1) * c.ESL], c.ESL,
                                extra_guard=guard)
            for tc in range(c.NQC):
                bank = self.ps[4 + tc % 2]
                sc = c.NSC - c.NQC + tc
                for dc in range(c.DCH):
                    wl = []
                    kq = es * c.NQC + tc
                    if dc == 0:
                        wl = [("wc0", em.c("wc0")), ("wc1", em.c("wc1")), ("bkfree", max(0, em.c("bkfree") - 1))] + (
                             [self.slw("xqw", kq - 2)] if kq >= 2 else [])
                    em.op("tensor", lambda t, tc=tc, dc=dc, bank=bank, wqv=wqv: t.matmul(
                        bank[:, 0:c.ESL],
                        xfm[:, dc * c.Tq + tc * 128:][:, 0:128],
                        wqv[:, dc * c.ESL:(dc + 1) * c.ESL],
                        start=(dc == 0), stop=(dc == c.DCH - 1)), waits=wl, inc="mm")
                qo = qstage[:, (tc % 2) * c.ESL:][:, 0:c.ESL]
                self._qk_epilogue(bank, qo, sc, is_q=True)
                self.dma_sl("xqw", es * c.NQC + tc,
                            self.dsc["XQ"][tc * 128:(tc + 1) * 128, es * c.ESL:(es + 1) * c.ESL],
                            qo, waits=[("epid", em.c("epid"))])
                self.pc_step(1)
            mm_after_es.append(em.c("mm"))
        ab.freebuf("wq")
        ab.freebuf("kstage")
        ab.freebuf("qstage")
        ab.freebuf("xfm")
        af.freebuf("ropebuf")
        af.freebuf("costab")
        af.freebuf("sintab")

    def _qk_epilogue(self, bank, obf, sc, is_q):
        c, em = self.cfg, self.em
        W = c.ESL if is_q else c.KVW
        nh = W // c.Dh
        rope = self.af.view("ropebuf")
        rbuf = rope[:, 0:W]
        ss = rope[:, W:W + nh]
        inv = rope[:, W + 16:W + 16 + nh]
        gbc = self.gq_bc if is_q else self.gk_bc
        cosv = self.costab[:, sc * c.WTAB:(sc + 1) * c.WTAB]
        sinv = self.sintab[:, sc * c.WTAB:(sc + 1) * c.WTAB]
        tmp = self.misc[:, 0:c.Dh // 2]
        ng2 = W // c.WTAB
        for g2 in range(ng2):
            src = bank[:, g2 * c.WTAB:(g2 + 1) * c.WTAB]
            dst = rbuf[:, g2 * c.WTAB:(g2 + 1) * c.WTAB]
            self.vop(lambda v, src=src, dst=dst: v.tensor_tensor(dst, src, cosv, AL.mult),
                     waits=[("mm", em.c("mm")), ("epid", em.c("epid"))])
            for h2 in range(2):
                for half in range(2):
                    sh = bank[:, g2 * c.WTAB + h2 * c.Dh + (1 - half) * (c.Dh // 2):][:, 0:c.Dh // 2]
                    sv = sinv[:, h2 * c.Dh + half * (c.Dh // 2):][:, 0:c.Dh // 2]
                    dv = dst[:, h2 * c.Dh + half * (c.Dh // 2):][:, 0:c.Dh // 2]
                    last_read = (g2 == ng2 - 1 and h2 == 1 and half == 1)
                    self.vop(lambda v, sh=sh, sv=sv: v.tensor_tensor(tmp, sh, sv, AL.mult),
                             waits=["ve"])
                    self.vop(lambda v, dv=dv: v.tensor_tensor(dv, dv, tmp, AL.add), waits=["ve"])
                    if last_read:
                        self.em.op("vector", lambda v: None, waits=["ve"], inc="bkfree")
        for h in range(nh):
            em.op("scalar", lambda s, h=h: s.activation(
                self.misc[:, 64 + (h % 2) * c.Dh:64 + (h % 2 + 1) * c.Dh],
                rbuf[:, h * c.Dh:(h + 1) * c.Dh], AF.Square,
                accum_out=ss[:, h:h + 1]), waits=["ve", ("ac", max(0, self.em.c("ac") - 1))],
                inc="ac")
        em.op("scalar", lambda s: s.activation(inv[:, 0:nh], ss[:, 0:nh], AF.Sqrt),
              waits=["ac"], inc="ac")
        self.vop(lambda v: v.tensor_scalar(inv[:, 0:nh], inv[:, 0:nh], 1e-6, None, AL.add),
                 waits=["ac"])
        self.vop(lambda v: v.reciprocal(inv[:, 0:nh], inv[:, 0:nh]), waits=["ve"])
        for h in range(nh):
            gv = gbc[:, (h % 2) * c.Dh:(h % 2 + 1) * c.Dh]
            self.vop(lambda v, h=h, gv=gv: v.scalar_tensor_tensor(
                obf[:, h * c.Dh:(h + 1) * c.Dh], rbuf[:, h * c.Dh:(h + 1) * c.Dh],
                inv[:, h:h + 1], gv, AL.mult, AL.mult), waits=["ve"])
        self.em.op("vector", lambda v: None, waits=["ve"], inc="epid")

    def _p3(self):
        self.soft_barrier()
        c, em, ab, af = self.cfg, self.em, self.ab, self.af
        hpc0 = 128 // self.cfg.Dh
        self.yh = ab.alloc("yh", (c.H // hpc0) * c.Tq)
        self.wgb = ab.alloc("wgb", c.DCH * c.E)
        self.load_cast(self.wgb, self.din["Wg"], c.DCH, c.E)
        self.bgb = af.alloc("bgb", c.E)
        self.dma(self.bgb[:, :], self.din["bgrep"], inc="ldbg")
        self.mask_of = {}
        for ic in range(c.NIC):
            for jb in self._jblocks(ic):
                c0 = c.Tq + ic * c.ICW - jb * 128
                if self._mask_kind(c0) == "partial" and c0 not in self.mask_of:
                    self.mask_of[c0] = len(self.mask_of)
        self.maskbuf = ab.alloc("masks", max(len(self.mask_of), 1) * c.ICW)
        for c0, mi in self.mask_of.items():
            mv = self.maskbuf[:, mi * c.ICW:(mi + 1) * c.ICW]
            self.vop(lambda v, mv=mv: v.memset(mv, 1.0))
            em.op("gpsimd", lambda g, mv=mv, c0=c0: g.affine_select(
                mv, mv, pattern=[[1, c.ICW]], compare_op=AL.is_ge, fill=0.0,
                base=c0, channel_multiplier=-1), waits=[("ve", em.c("ve"))], inc="gp")
            em.op("gpsimd", lambda g, mv=mv, c0=c0: g.affine_select(
                mv, mv, pattern=[[-1, c.ICW]], compare_op=AL.is_ge, fill=0.0,
                base=c.WIN - c0, channel_multiplier=1), waits=["gp"], inc="gp")
        qfm = ab.alloc("qfm", 2 * c.G * c.Tq)
        kfm = ab.alloc("kfm", 2 * c.S)
        hpc = 128 // c.Dh
        self.hpc = hpc
        self.rbcbuf = af.alloc("rbcbuf", 2 * c.ICW)
        ebuf = ab.alloc("ebuf", 4 * c.ICW)
        first = True
        mme_after = []
        epi_val = {}
        pend_epi = [None]
        prev_flush = [0]
        for hk in range(c.HK):
            wstage = [("mmE", mme_after[hk - 2])] if hk >= 2 else []
            kfv = kfm[:, (hk % 2) * c.S:][:, 0:c.S]
            qfv = qfm[:, (hk % 2) * c.G * c.Tq:][:, 0:c.G * c.Tq]
            self.dma_t(kfv[0:c.Dh, :], self.dsc["XK"][:, hk * c.Dh:(hk + 1) * c.Dh],
                       waits=self.sl_full("xkw") + wstage, inc=f"kfr{hk % 2}")
            for g in range(c.G):
                h = hk * c.G + g
                self.dma_t(qfv[0:c.Dh, g * c.Tq:(g + 1) * c.Tq],
                           self.dsc["XQ"][:, h * c.Dh:(h + 1) * c.Dh],
                           waits=self.sl_full("xqw") + wstage, inc=f"qfr{hk % 2}")
            for ic in range(c.NIC):
                jbs = self._jblocks(ic)
                njb = len(jbs)
                for g in range(c.G):
                    h = hk * c.G + g
                    gi = (hk * c.NIC + ic) * c.G + g
                    ps_y = self.ps[2 if gi % 2 == 0 else 5]
                    ps_r = self.ps[3 if gi % 2 == 0 else 7]
                    qv = qfv[0:c.Dh, g * c.Tq + ic * c.ICW:][:, 0:c.ICW]
                    consume_spec = {}
                    mme_blk = {}

                    def dnpv(k, jbs=jbs, njb=njb, ps_r=ps_r, ps_y=ps_y, gi=gi, hk=hk,
                             consume_spec=consume_spec, mme_blk=mme_blk):
                        jb = jbs[k]
                        ev = ebuf[:, (k % 4) * c.ICW:][:, 0:c.ICW]
                        wl = [consume_spec[k]]
                        if k == 0 and gi >= 2:
                            wl.append(("ve2b", epi_val[gi - 2]))
                        em.op("tensor", lambda t, ev=ev, k=k, njb=njb, ps_r=ps_r: t.matmul(
                            ps_r[0:1, 0:c.ICW], self.ones_bf[:, 0:1], ev,
                            start=(k == 0), stop=(k == njb - 1)), waits=wl, inc="mmE")
                        vv = self.vtm[:, jb * c.KVW + hk * c.Dh:][:, 0:c.Dh]
                        em.op("tensor", lambda t, ev=ev, vv=vv, k=k, njb=njb, ps_y=ps_y: t.matmul(
                            ps_y[0:c.Dh, 0:c.ICW], vv, ev,
                            start=(k == 0), stop=(k == njb - 1)), inc="mmE")
                        mme_blk[k] = em.c("mmE")

                    for bi, jb in enumerate(jbs):
                        sb = self.ps[(0, 1, 4)[bi % 3]]
                        ev = ebuf[:, (bi % 4) * c.ICW:][:, 0:c.ICW]
                        kv = kfv[0:c.Dh, jb * 128:][:, 0:128]
                        wl = [("ac2", max(0, em.c("ac2") - 2))]
                        if bi == 0:
                            wl += [(f"qfr{hk % 2}", em.c(f"qfr{hk % 2}")),
                                   (f"kfr{hk % 2}", em.c(f"kfr{hk % 2}"))]
                        if first:
                            wl += [("gp", em.c("gp")), ("ve", em.c("ve"))]
                            first = False
                        em.op("tensor", lambda t, sb=sb, kv=kv, qv=qv: t.matmul(
                            sb[:, 0:c.ICW], kv, qv, start=True, stop=True),
                            waits=wl, inc="mm")
                        ebguard = ([("mmE", mme_blk[bi - 4])] if bi >= 4
                                   else [("mmE", prev_flush[0])])
                        em.op("scalar", lambda s, sb=sb, ev=ev, jb=jb: s.activation(
                            ev, sb[:, 0:c.ICW], AF.Exp, bias=self.negBpad[:, jb:jb + 1]),
                            waits=[("mm", em.c("mm"))] + ebguard, inc="ac2")
                        c0 = c.Tq + ic * c.ICW - jb * 128
                        if self._mask_kind(c0) == "partial":
                            mi = self.mask_of[c0]
                            em.op("vector", lambda v, ev=ev, mi=mi: v.tensor_tensor(
                                ev, ev, self.maskbuf[:, mi * c.ICW:(mi + 1) * c.ICW], AL.mult),
                                waits=[("ac2", em.c("ac2"))], inc="ve2")
                            consume_spec[bi] = ("ve2", em.c("ve2"))
                        else:
                            consume_spec[bi] = ("ac2", em.c("ac2"))
                        if bi == 2 and pend_epi[0] is not None:
                            pend_epi[0]()
                            pend_epi[0] = None
                        if bi >= 2:
                            dnpv(bi - 2)
                    if pend_epi[0] is not None:
                        pend_epi[0]()
                        pend_epi[0] = None
                    for k in range(max(0, njb - 2), njb):
                        dnpv(k)
                    prev_flush[0] = em.c("mmE")
                    rrow = (self.misc[0:1, 0:c.ICW] if gi % 2 == 0
                            else self.misc[0:1, 544:544 + c.ICW])
                    rbc = self.rbcbuf[:, (gi % 2) * c.ICW:][:, 0:c.ICW]
                    em.op("vector", lambda v, rrow=rrow, ps_r=ps_r: v.reciprocal(
                        rrow, ps_r[0:1, 0:c.ICW]), waits=[("mmE", em.c("mmE"))],
                        inc="ve2b")
                    ve2b_after_recip = em.c("ve2b")
                    yo = self.yh[(h % hpc) * c.Dh:(h % hpc + 1) * c.Dh,
                                 (h // hpc) * c.Tq + ic * c.ICW:][:, 0:c.ICW]

                    def epi(gi=gi, rrow=rrow, rbc=rbc, yo=yo, ps_y=ps_y,
                            var=ve2b_after_recip):
                        em.op("tensor", lambda t, rrow=rrow: t.matmul(
                            self.ps[6][:, 0:c.ICW], self.onesr_f[0:1, :], rrow,
                            start=True, stop=True), waits=[("ve2b", var)], inc="mmb")
                        em.op("vector", lambda v, rbc=rbc: v.tensor_copy(
                            rbc, self.ps[6][:, 0:c.ICW]), waits=[("mmb", em.c("mmb"))],
                            inc="ve2b")
                        em.op("vector", lambda v, yo=yo, rbc=rbc, ps_y=ps_y: v.tensor_tensor(
                            yo, ps_y[0:c.Dh, 0:c.ICW], rbc[0:c.Dh, :], AL.mult),
                            waits=[("ve2b", em.c("ve2b"))], inc="ve2b")
                        epi_val[gi] = em.c("ve2b")
                    pend_epi[0] = epi
            mme_after.append(em.c("mmE"))
        if pend_epi[0] is not None:
            pend_epi[0]()
            pend_epi[0] = None
        ab.freebuf("ebuf")
        ab.freebuf("qfm")
        ab.freebuf("kfm")
        ab.freebuf("vtm")
        ab.freebuf("masks")
        af.freebuf("rbcbuf")

    def _p4(self):
        self.soft_barrier()
        c, em, ab, af = self.cfg, self.em, self.ab, self.af
        NEC = c.QW // 128
        wo = ab.alloc("wo", NEC * c.DSW)
        xjit = af.alloc("xjit", 2 * c.DSW)
        x1st = af.alloc("x1st", 2 * c.DSW)
        for ds in range(c.NDS):
            self.load_cast_cols(wo, self.din["woT"][:, ds * c.DSW:(ds + 1) * c.DSW], c.DSW,
                                nch=NEC, extra_guard=[("mm", em.c("mm"))])
            for tc in range(c.NQC):
                bank = self.ps[4 + tc % 2]
                kx = ds * c.NQC + tc
                xj = xjit[:, (tc % 2) * c.DSW:][:, 0:c.DSW]
                self.dma_sl("xjl", kx, xj,
                            self.din["x_slab"][c.Tq + tc * 128:c.Tq + (tc + 1) * 128,
                                               ds * c.DSW:(ds + 1) * c.DSW],
                            waits=[("x1ve", max(0, em.c("x1ve") - 1))])
                for ec in range(NEC):
                    wl = []
                    if ec == 0:
                        wl = [("wc0", em.c("wc0")), ("wc1", em.c("wc1")),
                              ("ve2b", em.c("ve2b")),
                              ("x1ve", max(0, em.c("x1ve") - 1))]
                    em.op("tensor", lambda t, tc=tc, ec=ec, bank=bank: t.matmul(
                        bank[:, 0:c.DSW],
                        self.yh[:, ec * c.Tq + tc * 128:][:, 0:128],
                        wo[:, ec * c.DSW:(ec + 1) * c.DSW],
                        start=(ec == 0), stop=(ec == NEC - 1)), waits=wl, inc="mm")
                xo = x1st[:, (tc % 2) * c.DSW:][:, 0:c.DSW]
                em.op("vector", lambda v, xj=xj, xo=xo, bank=bank: v.tensor_tensor(
                    xo, xj, bank[:, 0:c.DSW], AL.add),
                    waits=[("mm", em.c("mm")), self.slw("xjl", kx)] + (
                          [self.slw("x1pw", kx - 2)] if kx >= 2 else []), inc="x1ve")
                self.dma_sl("x1pw", kx, self.dsc["X1PARK"][tc * 128:(tc + 1) * 128,
                                                           ds * c.DSW:(ds + 1) * c.DSW], xo,
                            waits=[("x1ve", em.c("x1ve"))])
        ab.freebuf("wo")
        af.freebuf("xjit")
        af.freebuf("x1st")
        ab.freebuf("yh")

    def _p5(self):
        self.soft_barrier()
        c, em = self.cfg, self.em
        ab, af = self.ab, self.af
        xlnbf = ab.alloc("x2bf", 2 * c.D)
        x1rd = af.alloc("x1rd", 2 * c.D)
        st = self.lnst
        nbn = (c.D + 511) // 512
        for tc in range(c.NQC):
            xt = x1rd[:, (tc % 2) * c.D:][:, 0:c.D]
            ob = xlnbf[:, (tc % 2) * c.D:][:, 0:c.D]
            self.dma_sl("x1rd", tc, xt, self.dsc["X1PARK"][tc * 128:(tc + 1) * 128, :],
                        waits=[self.slw("x1pw", (c.NDS - 1) * c.NQC + tc),
                               ("ve", max(0, em.c("ve") - 4))])
            for i in range(nbn):
                w = min(512, c.D - i * 512)
                self.vop(lambda v, xt=xt, i=i, w=w: v.bn_stats(
                    st[:, 6 * i:6 * i + 6], xt[:, i * 512:i * 512 + w]),
                    waits=[self.slw("x1rd", tc)])
            self.vop(lambda v: v.bn_aggr(st[:, 24:26], st[:, 0:6 * nbn]), waits=["ve"])
            em.op("scalar", lambda s: s.activation(st[:, 26:27], st[:, 25:26], AF.Sqrt,
                                                   bias=self.epsln[:, :]), waits=["ve"], inc="ac")
            self.vop(lambda v: v.reciprocal(st[:, 26:27], st[:, 26:27]), waits=["ac"])
            self.vop(lambda v: v.tensor_scalar(st[:, 27:28], st[:, 24:25], -1.0, None, AL.mult),
                     waits=["ve"])
            self.vop(lambda v, xt=xt, ob=ob: v.tensor_scalar(
                ob, xt, st[:, 27:28], st[:, 26:27], AL.add, AL.mult),
                waits=["ve"] + ([self.slw("xt2w", tc - 2)] if tc >= 2 else []))
            self.dma_sl("xt2w", tc, self.dsc["XT2loc"][tc * 128:(tc + 1) * 128, :], ob,
                        waits=[("ve", em.c("ve"))])
        x2fm = ab.alloc("x2fm", 2 * c.DCH * 128)
        ab.freebuf("x2bf")
        af.freebuf("x1rd")
        tpc = c.AGCH // 128
        for cch in range(c.NAGC):
            rows = c.n_cores * c.AGCH
            em.op("gpsimd", lambda g, cch=cch, rows=rows: g.collective_compute(
                "AllGather", AL.bypass, replica_groups=[list(range(c.n_cores))],
                ins=[self.dsc["XT2loc"][cch * c.AGCH:(cch + 1) * c.AGCH, :]],
                outs=[self.dsc["XT2allC"][cch * rows:(cch + 1) * rows, :]]),
                waits=[self.slw("xt2w", cch * tpc + i) for i in range(tpc)], inc="cc")
        wgb = self.wgb
        bgb = self.bgb
        lg = af.alloc("lgbuf", c.NQC * c.E)
        for tc in range(c.NQC):
            xv = x2fm[:, (tc % 2) * c.DCH * 128:][:, 0:c.DCH * 128]
            x2sem = f"x2fr{tc % 2}"
            for dc in range(c.DCH):
                self.em.op("sync", lambda s, xv=xv, dc=dc, tc=tc: s.dma_start(
                    out=xv[:, dc * 128:(dc + 1) * 128],
                    in_=self.dsc["XT2loc"][tc * 128:(tc + 1) * 128, dc * 128:(dc + 1) * 128],
                    transpose=True),
                    waits=[self.slw("xt2w", tc),
                           ("gmm", max(0, em.c("gmm") - c.DCH))],
                    inc=x2sem, dma=True)
            if "x2fmdbg" in self.taps:
                for dc in range(c.DCH):
                    self.dma(self.dsc["X2FMDBG"][dc * 128:(dc + 1) * 128,
                                                 tc * 128:(tc + 1) * 128],
                             xv[:, dc * 128:(dc + 1) * 128],
                             waits=[(x2sem, em.c(x2sem))], inc="dbgw")
            for dc in range(c.DCH):
                wl = [(x2sem, em.c(x2sem)), ("wc0", em.c("wc0")), ("wc1", em.c("wc1")),
                      ("lgve", em.c("lgve"))] if dc == 0 else []
                if dc == 0 and "x2fmdbg" in self.taps:
                    wl.append(("dbgw", em.c("dbgw")))
                em.op("tensor", lambda t, xv=xv, dc=dc: t.matmul(
                    self.ps[7][:, 0:c.E], xv[:, dc * 128:(dc + 1) * 128],
                    wgb[:, dc * c.E:(dc + 1) * c.E],
                    start=(dc == 0), stop=(dc == c.DCH - 1)), waits=wl, inc="gmm")
            self.vop(lambda v, tc=tc: v.tensor_tensor(
                lg[:, tc * c.E:(tc + 1) * c.E], self.ps[7][:, 0:c.E], bgb[:, :], AL.add),
                waits=[("gmm", em.c("gmm")), ("ldbg", em.c("ldbg"))], inc="lgve")
        self.dma(self.dsc["LGloc"].rearrange("(a p) e -> p a e", p=128),
                 lg[:, :].rearrange("p (a e) -> p a e", e=c.E),
                 waits=[("lgve", em.c("lgve"))], inc="lgw")
        em.op("gpsimd", lambda g: g.collective_compute(
            "AllGather", AL.bypass, replica_groups=[list(range(c.n_cores))],
            ins=[self.dsc["LGloc"]], outs=[self.dsc["LGall"]]),
            waits=[("lgw", em.c("lgw"))], inc="cc")
        ab.freebuf("wgb")
        ab.freebuf("x2fm")
        af.freebuf("lgbuf")
        af.freebuf("bgb")

    def _p5b(self):
        self.soft_barrier(exclude=("cc",))
        c, em, ab, af = self.cfg, self.em, self.ab, self.af
        E, NCH = c.E, c.NCH
        P = af.alloc("planes", NCH * E)
        self.dma(P[:, :].rearrange("p (ch e) -> p ch e", e=E),
                 self.dsc["LGall"].rearrange("(ch p) e -> p ch e", p=128),
                 waits=[("cc", em.c("cc"))], inc="pld")
        le = [P[:, e:NCH * E:E] for e in range(E)]
        Wk = af.alloc("rwork", NCH * 33 + 8 * E + 32)
        off = [0]

        def buf(n):
            v = Wk[:, off[0]:off[0] + n]
            off[0] += n
            return v
        m1, m2, u, p1, p2, tmp, seen = (buf(NCH) for _ in range(7))
        first1 = [buf(NCH) for _ in range(E)]
        first2 = [buf(NCH) for _ in range(E)]
        score = [buf(NCH) for _ in range(E)]

        def R(fn, waits=None):
            em.op("vector", fn, waits=waits if waits is not None else [("rve", em.c("rve"))],
                  inc="rve")
        R(lambda v: v.tensor_tensor(m1, le[0], le[1], AL.max), waits=[("pld", em.c("pld"))])
        for e in range(2, E):
            R(lambda v, e=e: v.tensor_tensor(m1, m1, le[e], AL.max))
        R(lambda v: v.memset(seen, 0.0))
        for e in range(E):
            R(lambda v, e=e: v.tensor_tensor(first1[e], le[e], m1, AL.is_equal))
            R(lambda v: v.tensor_scalar(tmp, seen, -1.0, 1.0, AL.mult, AL.add))
            R(lambda v, e=e: v.tensor_tensor(first1[e], first1[e], tmp, AL.mult))
            R(lambda v, e=e: v.tensor_tensor(seen, seen, first1[e], AL.add))
        R(lambda v: v.memset(m2, -1e30))
        for e in range(E):
            R(lambda v, e=e: v.scalar_tensor_tensor(score[e], first1[e], -1e30, le[e],
                                                    AL.mult, AL.add))
            R(lambda v, e=e: v.tensor_tensor(m2, m2, score[e], AL.max))
        R(lambda v: v.memset(seen, 0.0))
        for e in range(E):
            R(lambda v, e=e: v.tensor_tensor(first2[e], score[e], m2, AL.is_equal))
            R(lambda v: v.tensor_scalar(tmp, seen, -1.0, 1.0, AL.mult, AL.add))
            R(lambda v, e=e: v.tensor_tensor(first2[e], first2[e], tmp, AL.mult))
            R(lambda v, e=e: v.tensor_tensor(seen, seen, first2[e], AL.add))
        R(lambda v: v.tensor_tensor(u, m2, m1, AL.subtract))
        em.op("scalar", lambda s: s.activation(u, u, AF.Exp),
              waits=[("rve", em.c("rve"))], inc="rac")
        R(lambda v: v.tensor_scalar(p1, u, 1.0, None, AL.add), waits=[("rac", em.c("rac"))])
        R(lambda v: v.reciprocal(p1, p1))
        R(lambda v: v.tensor_tensor(p2, u, p1, AL.mult))
        for e in range(E):
            R(lambda v, e=e: v.tensor_tensor(score[e], first1[e], p1, AL.mult))
            R(lambda v, e=e: v.tensor_tensor(tmp, first2[e], p2, AL.mult))
            R(lambda v, e=e: v.tensor_tensor(score[e], score[e], tmp, AL.add))
            R(lambda v, e=e: v.tensor_tensor(tmp, first1[e], first2[e], AL.add))
            R(lambda v, e=e: v.tensor_scalar(tmp, tmp, -1.0, 1.0, AL.mult, AL.add))
            R(lambda v, e=e: v.scalar_tensor_tensor(score[e], tmp, -1.0, score[e],
                                                    AL.mult, AL.add))
        lo, hi, tt, cnt, ge, tb = (buf(E) for _ in range(6))
        maskb = ab.alloc("selmaskb", NCH * E)
        mb = [maskb[:, e * NCH:(e + 1) * NCH] for e in range(E)]
        R(lambda v: v.memset(lo[0:1, :], 0.0))
        R(lambda v: v.memset(hi[0:1, :], 1.0))
        for it in range(c.n_search):
            R(lambda v: v.tensor_tensor(tt[0:1, :], lo[0:1, :], hi[0:1, :], AL.add))
            R(lambda v: v.tensor_scalar(tt[0:1, :], tt[0:1, :], 0.5, None, AL.mult))
            em.op("tensor", lambda t: t.matmul(self.ps[6][:, 0:E], self.onesr_f[0:1, :],
                                               tt[0:1, :], start=True, stop=True),
                  waits=[("rve", em.c("rve"))], inc="rmm")
            R(lambda v: v.tensor_copy(tb, self.ps[6][:, 0:E]), waits=[("rmm", em.c("rmm"))])
            for e in range(E):
                R(lambda v, e=e: v.tensor_scalar(mb[e], score[e], tb[:, e:e + 1], 0.0,
                                                 AL.is_ge, AL.add,
                                                 accum_out=cnt[:, e:e + 1]))
            em.op("tensor", lambda t: t.matmul(self.ps[6][0:1, 0:E], self.ones_f[:, 0:1],
                                               cnt, start=True, stop=True),
                  waits=[("rve", em.c("rve"))], inc="rmm")
            R(lambda v: v.tensor_scalar(ge[0:1, :], self.ps[6][0:1, 0:E], float(c.CAP),
                                        None, AL.is_ge), waits=[("rmm", em.c("rmm"))])
            R(lambda v: v.tensor_tensor(cnt[0:1, :], tt[0:1, :], lo[0:1, :], AL.subtract))
            R(lambda v: v.tensor_tensor(cnt[0:1, :], cnt[0:1, :], ge[0:1, :], AL.mult))
            R(lambda v: v.tensor_tensor(lo[0:1, :], lo[0:1, :], cnt[0:1, :], AL.add))
            R(lambda v: v.tensor_tensor(cnt[0:1, :], tt[0:1, :], hi[0:1, :], AL.subtract))
            R(lambda v: v.tensor_scalar(ge[0:1, :], ge[0:1, :], -1.0, 1.0, AL.mult, AL.add))
            R(lambda v: v.tensor_tensor(cnt[0:1, :], cnt[0:1, :], ge[0:1, :], AL.mult))
            R(lambda v: v.tensor_tensor(hi[0:1, :], hi[0:1, :], cnt[0:1, :], AL.add))
            self.pc_step(1)
        em.op("tensor", lambda t: t.matmul(self.ps[6][:, 0:E], self.onesr_f[0:1, :],
                                           lo[0:1, :], start=True, stop=True),
              waits=[("rve", em.c("rve"))], inc="rmm")
        R(lambda v: v.tensor_copy(tb, self.ps[6][:, 0:E]), waits=[("rmm", em.c("rmm"))])
        rpl = [buf(NCH) for _ in range(E)]
        mpl = [buf(NCH) for _ in range(E)]
        ctot = buf(NCH)
        for e in range(E):
            R(lambda v, e=e: v.tensor_scalar(mb[e], score[e], tb[:, e:e + 1], None, AL.is_ge))
            em.op("tensor", lambda t, e=e: t.matmul(self.ps[6][:, 0:NCH], self.tri_bf[:, :],
                                                    mb[e], start=True, stop=True),
                  waits=[("rve", em.c("rve"))], inc="rmm")
            em.op("tensor", lambda t, e=e: t.matmul(self.ps[7][0:1, 0:NCH],
                                                    self.ones_bf[:, 0:1], mb[e],
                                                    start=True, stop=True), inc="rmm")
            R(lambda v, e=e: v.tensor_copy(rpl[e], self.ps[6][:, 0:NCH]),
              waits=[("rmm", em.c("rmm"))])
            R(lambda v: v.tensor_copy(ctot[0:1, :], self.ps[7][0:1, 0:NCH]))
            sh = 1
            while sh < NCH:
                R(lambda v, sh=sh: v.tensor_tensor(ctot[0:1, sh:NCH], ctot[0:1, sh:NCH],
                                                   ctot[0:1, 0:NCH - sh], AL.add))
                sh *= 2
            em.op("tensor", lambda t: t.matmul(self.ps[6][:, 0:NCH], self.onesr_f[0:1, :],
                                               ctot[0:1, 0:NCH], start=True, stop=True),
                  waits=[("rve", em.c("rve"))], inc="rmm")
            R(lambda v, e=e: v.tensor_tensor(rpl[e][:, 1:NCH], rpl[e][:, 1:NCH],
                                             self.ps[6][:, 0:NCH - 1], AL.add),
              waits=[("rmm", em.c("rmm"))])
            R(lambda v, e=e: v.tensor_scalar(mpl[e], rpl[e], float(c.CAP), None, AL.is_lt))
            R(lambda v, e=e: v.tensor_tensor(mpl[e], mpl[e], mb[e], AL.mult))
            self.pc_step(2)
        ab.freebuf("selmaskb")
        rec = buf(4 * NCH)
        s0 = rec[:, 0:4 * NCH:4]
        q0 = rec[:, 1:4 * NCH:4]
        s1 = rec[:, 2:4 * NCH:4]
        q1 = rec[:, 3:4 * NCH:4]
        for (sv, qv, fl, pv) in ((s0, q0, first1, p1), (s1, q1, first2, p2)):
            R(lambda v, sv=sv: v.memset(sv, 0.0))
            R(lambda v, qv=qv: v.memset(qv, 0.0))
            for e in range(E):
                R(lambda v, e=e, fl=fl: v.tensor_tensor(tmp, rpl[e], fl[e], AL.mult))
                R(lambda v, e=e, fl=fl: v.scalar_tensor_tensor(u, fl[e], float(c.CAP * e),
                                                               tmp, AL.mult, AL.add))
                R(lambda v, sv=sv: v.tensor_tensor(sv, sv, u, AL.add))
                R(lambda v, e=e, fl=fl: v.tensor_tensor(tmp, fl[e], mpl[e], AL.mult))
                R(lambda v, pv=pv: v.tensor_tensor(tmp, tmp, pv, AL.mult))
                R(lambda v, qv=qv: v.tensor_tensor(qv, qv, tmp, AL.add))
                self.pc_step(1)
        self.dma(self.dsc["RECd"].rearrange("(p ch) r -> p (ch r)", p=128), rec[:, 0:4 * NCH],
                 waits=[("rve", em.c("rve"))], inc="recw")
        rk = af.alloc("rkcol", 4)
        self.dma(rk[:, 0:1], self.din["rankcol"], inc="ldrk")
        self.dma(rk[:, 1:2], self.din["hcol"], inc="ldrk2")
        self.dma(rk[:, 2:3], self.din["ecol"], inc="ldrk3")
        pio = af.alloc("pio", 1)
        self.iota_f32(pio[:, 0:1], [[0, 1]], 0, 1, waits=[("rve", em.c("rve"))])
        idxf = af.alloc("idxf", c.NQC + 8)
        idxu = self.ai.alloc("idxu", max(c.NQC, 2))
        R(lambda v: v.tensor_scalar(idxf[:, 0:1], pio[:, 0:1], float(NCH), None, AL.mult),
          waits=[("iocast", em.c("iocast")), ("ldrk", em.c("ldrk")), ("ldrk2", em.c("ldrk2")), ("ldrk3", em.c("ldrk3"))])
        R(lambda v: v.scalar_tensor_tensor(idxf[:, 0:1], rk[:, 0:1], float(c.NQC),
                                           idxf[:, 0:1], AL.mult, AL.add))
        for k in range(c.NQC):
            R(lambda v, k=k: v.tensor_scalar(idxf[:, 1 + k:2 + k], idxf[:, 0:1], float(k),
                                             None, AL.add))
            self.f2i(idxu[:, k:k + 1], idxf[:, 1 + k:2 + k],
                     waits=[("rve", em.c("rve"))], inc="rve")
            em.op("gpsimd", lambda g, k=k: g.indirect_dma_start(
                out=self.myrec[:, 4 * k:4 * k + 4], out_offset=None, in_=self.dsc["RECd"],
                in_offset=IndirectOffsetOnAxis(ap=idxu[:, k:k + 1].bitcast(U32), axis=0)),
                waits=[("rve", em.c("rve")), ("recw", em.c("recw"))], inc="ind", dma=True)
        neg1 = self.misc[:, 0:(c.E * c.LSLOT * 4) // 128]
        R(lambda v, neg1=neg1: v.memset(neg1, -1.0))
        self.dma(self.dsc["LISTloc"].rearrange("(p a) r -> p (a r)", p=128), neg1,
                 waits=[("rve", em.c("rve"))], inc="linit")
        tokb = af.alloc("tokb", 1)
        R(lambda v: v.scalar_tensor_tensor(tokb[:, 0:1], rk[:, 0:1], float(c.Tq), pio[:, 0:1],
                                           AL.mult, AL.add))
        sc_idx = self.ai.alloc("scidx", 2)
        sval = af.alloc("sval", 8)
        for j in range(2):
            for k in range(c.NQC):
                sv = self.myrec[:, 4 * k + 2 * j:4 * k + 2 * j + 1]
                qv = self.myrec[:, 4 * k + 2 * j + 1:4 * k + 2 * j + 2]
                R(lambda v, qv=qv: v.tensor_scalar(sval[:, 0:1], qv, 0.0, None, AL.is_gt),
                  waits=[("rve", em.c("rve")), ("ind", em.c("ind")),
                         ("lsc", em.c("lsc")), ("linit", em.c("linit"))])
                R(lambda v, sv=sv: v.tensor_tensor(sval[:, 1:2], sv, sval[:, 0:1], AL.mult))
                R(lambda v: v.tensor_scalar(sval[:, 4:5], sval[:, 1:2], 1.0 / c.CAP,
                                            None, AL.mult))
                self.f2i(sc_idx[:, 0:1], sval[:, 4:5], waits=[("rve", em.c("rve"))], inc="rve")
                R(lambda v: v.tensor_copy(sval[:, 6:7], sc_idx[:, 0:1]))
                R(lambda v: v.tensor_tensor(sval[:, 7:8], sval[:, 4:5], sval[:, 6:7],
                                            AL.subtract))
                R(lambda v: v.tensor_scalar(sval[:, 7:8], sval[:, 7:8], 0.0, None, AL.is_lt))
                R(lambda v: v.tensor_tensor(sval[:, 4:5], sval[:, 6:7], sval[:, 7:8],
                                            AL.subtract))
                R(lambda v: v.scalar_tensor_tensor(sval[:, 1:2], sval[:, 4:5],
                                                   float(c.LSLOT - c.CAP), sval[:, 1:2],
                                                   AL.mult, AL.add))
                R(lambda v: v.tensor_tensor(sval[:, 1:2], sval[:, 1:2], sval[:, 0:1], AL.mult))
                R(lambda v: v.tensor_scalar(sval[:, 2:3], sval[:, 0:1], -1.0, 1.0,
                                            AL.mult, AL.add))
                R(lambda v: v.tensor_scalar(sval[:, 3:4], pio[:, 0:1],
                                            float(c.E * c.LSLOT - 128), None, AL.add))
                R(lambda v: v.tensor_tensor(sval[:, 2:3], sval[:, 2:3], sval[:, 3:4], AL.mult))
                R(lambda v: v.tensor_tensor(sval[:, 1:2], sval[:, 1:2], sval[:, 2:3], AL.add))
                self.f2i(sc_idx[:, 0:1], sval[:, 1:2], waits=[("rve", em.c("rve"))], inc="rve")
                R(lambda v, k=k: v.tensor_scalar(sval[:, 5:6], tokb[:, 0:1], float(k * 128),
                                                 None, AL.add))
                R(lambda v: v.memset(sval[:, 6:8], 0.0))
                R(lambda v: v.tensor_copy(sval[:, 4:5], sval[:, 5:6]))
                em.op("gpsimd", lambda g: g.indirect_dma_start(
                    out=self.dsc["LISTloc"],
                    out_offset=IndirectOffsetOnAxis(ap=sc_idx[:, 0:1].bitcast(U32), axis=0),
                    in_=sval[:, 4:8], in_offset=None),
                    waits=[("rve", em.c("rve")), ("linit", em.c("linit"))],
                    inc="lsc", dma=True)
        em.op("gpsimd", lambda g: g.collective_compute(
            "AllGather", AL.bypass, replica_groups=[list(range(c.n_cores))],
            ins=[self.dsc["LISTloc"]], outs=[self.dsc["LISTall"]]),
            waits=[("lsc", em.c("lsc"))], inc="cc")
        ncol = c.E * c.LSLOT * 4 // 128
        comb = af.alloc("comb", 2 * ncol)
        ca, cb = comb[:, 0:ncol], comb[:, ncol:2 * ncol]
        self.dma(ca, self.dsc["LISTall"][0:c.E * c.LSLOT, :].rearrange(
            "(p a) q -> p (a q)", p=128), waits=[("cc", em.c("cc"))], inc="pld2")
        for r in range(1, c.n_cores):
            base = r * c.E * c.LSLOT
            self.dma(cb, self.dsc["LISTall"][base:base + c.E * c.LSLOT, :].rearrange(
                "(p a) q -> p (a q)", p=128),
                waits=[("pld2", em.c("pld2")), ("rve", em.c("rve"))], inc="pld2")
            R(lambda v: v.tensor_tensor(ca, ca, cb, AL.max), waits=[("pld2", em.c("pld2"))])
        self.dma(self.dsc["LISTcmb"].rearrange("(p a) q -> p (a q)", p=128), ca,
                 waits=[("rve", em.c("rve"))], inc="lcw")
        af.freebuf("comb")
        myl = af.alloc("mylist", c.NQC * 4)
        R(lambda v: v.scalar_tensor_tensor(idxf[:, 0:1], rk[:, 2:3], float(c.LSLOT),
                                           pio[:, 0:1], AL.mult, AL.add))
        R(lambda v: v.scalar_tensor_tensor(idxf[:, 0:1], rk[:, 1:2], float(c.Tq),
                                           idxf[:, 0:1], AL.mult, AL.add))
        tokid = self.ai.alloc("tokid", c.NQC)
        for k in range(c.NQC):
            R(lambda v, k=k: v.tensor_scalar(idxf[:, 1 + k:2 + k], idxf[:, 0:1],
                                             float(k * 128), None, AL.add))
            self.f2i(idxu[:, k:k + 1], idxf[:, 1 + k:2 + k],
                     waits=[("rve", em.c("rve"))], inc="rve")
            em.op("gpsimd", lambda g, k=k: g.indirect_dma_start(
                out=myl[:, 4 * k:4 * k + 4], out_offset=None, in_=self.dsc["LISTcmb"],
                in_offset=IndirectOffsetOnAxis(ap=idxu[:, k:k + 1].bitcast(U32), axis=0)),
                waits=[("rve", em.c("rve")), ("lcw", em.c("lcw"))], inc="ind", dma=True)
        for k in range(c.NQC):
            R(lambda v, k=k: v.tensor_scalar(idxf[:, 1 + k:2 + k],
                                             myl[:, 4 * k:4 * k + 1], 0.0, None, AL.max),
              waits=[("ind", em.c("ind"))])
        rmf = af.alloc("rmf", 4 * c.NQC)
        self._ag_remap(rmf[:, 0:c.NQC], idxf[:, 1:1 + c.NQC],
                       rmf[:, c.NQC:4 * c.NQC], tokid[:, 0:c.NQC], R)
        self.f2i(tokid[:, 0:c.NQC], rmf[:, 0:c.NQC],
                 waits=[("rve", em.c("rve"))], inc="rve")
        af.freebuf("rmf")
        grow = ab.alloc("growbuf", 2 * c.D)
        for k in range(c.NQC):
            gv = grow[:, (k % 2) * c.D:][:, 0:c.D]
            em.op("gpsimd", lambda g, k=k, gv=gv: g.indirect_dma_start(
                out=gv, out_offset=None, in_=self.dsc["XT2allC"],
                in_offset=IndirectOffsetOnAxis(ap=tokid[:, k:k + 1].bitcast(U32), axis=0)),
                waits=[("rve", em.c("rve")), ("cc", em.c("cc"))] +
                      ([(f"xtokw{k % 2}", 16 * (k // 2))] if k >= 2 else []),
                inc=f"gind{k % 2}", dma=True)
            self.dma(self.dsc["XTOK"][k * 128:(k + 1) * 128, :], gv,
                     waits=[(f"gind{k % 2}", 16 * (k // 2 + 1))], inc=f"xtokw{k % 2}")
        ab.freebuf("growbuf")
        self.tokfm = ab.alloc("tokfm", c.DCH * c.Tq)
        for dc in range(c.DCH):
            self.dma_t(self.tokfm[:, dc * c.Tq:(dc + 1) * c.Tq],
                       self.dsc["XTOK"][:, dc * 128:(dc + 1) * 128],
                       waits=[("xtokw0", em.c("xtokw0")), ("xtokw1", em.c("xtokw1"))],
                       inc="tokfr")
        af.freebuf("planes")
        af.freebuf("rwork")
        af.freebuf("mylist")
        af.freebuf("idxf")
        af.freebuf("rkcol")
        af.freebuf("pio")
        af.freebuf("tokb")
        af.freebuf("sval")
        self.pc_step(len(self.pc_units) - self._pc_next)
        assert self._pc_next == len(self.pc_units)
        af.freebuf("pcstgf")
        ab.freebuf("pcstgb")
        self.w1b = ab.alloc("w1b", 4 * c.DCH * 128)
        self.w2b = ab.alloc("w2b", c.JGRP * c.D)
        guards = [("pcw0", em.c("pcw0")), ("pcw1", em.c("pcw1")),
                  ("xtokw0", em.c("xtokw0")), ("xtokw1", em.c("xtokw1")),
                  ("rve", em.c("rve")),
                  ("ind", em.c("ind"))]
        for j in range(min(4, c.NJ)):
            self._ffn_w1_load(j, guards)
        for j in range(min(c.JGRP, c.NJ)):
            self._ffn_w2_load(j, guards)

    def _ffn_w1_load(self, j, waits_extra=()):
        c, em = self.cfg, self.em
        wb = self.w1b[:, (j % 4) * c.DCH * 128:][:, 0:c.DCH * 128]
        h1, jq = j // c.JPH, j % c.JPH
        em.op("scalar", lambda s, wb=wb, h1=h1, jq=jq: s.dma_start(
            out=wb.rearrange("p (c j) -> p c j", c=c.DCH),
            in_=self.dsc["W1BFH"][h1 * c.DCH:(h1 + 1) * c.DCH, :,
                                  jq * 128:(jq + 1) * 128].rearrange("c p j -> p c j")),
            waits=[("ffmm", max(0, em.c("ffmm") - 3 * c.DCH * c.NTH))] + list(waits_extra),
            inc=f"w1l{j % 4}", dma=True)

    def _ffn_w2_load(self, j, waits_extra=()):
        c, em = self.cfg, self.em
        jj = j % c.JGRP
        em.op("gpsimd", lambda g, jj=jj, j=j: g.dma_start(
            out=self.w2b[:, jj * c.D:(jj + 1) * c.D], in_=self.dsc["W2BFH"][j]),
            waits=[("ffmm2", em.c("ffmm2"))] + list(waits_extra),
            inc=f"w2l{j % 2}", dma=True)

    def _p6(self):
        self.soft_barrier()
        c, em, ab, af = self.cfg, self.em, self.ab, self.af
        yacc = af.alloc("yacc", c.NQC * c.D)
        self.vop(lambda v: v.memset(yacc[:, :], 0.0), inc="yve")
        hgrp = ab.alloc("hgrp", c.JGRP * c.Tq)
        w1b, w2b = self.w1b, self.w2b
        for grp in range(c.NGRP):
            for jj in range(c.JGRP):
                j = grp * c.JGRP + jj
                wb = w1b[:, (j % 4) * c.DCH * 128:][:, 0:c.DCH * 128]
                if j >= 4:
                    self._ffn_w1_load(j)
                if j >= c.JGRP:
                    self._ffn_w2_load(j)
                for th in range(c.NTH):
                    bank = self.ps[(j * c.NTH + th) % 4]
                    for dc in range(c.DCH):
                        wl = []
                        if dc == 0:
                            wl = [(f"w1l{j % 4}", 16 * (j // 4 + 1)),
                                  ("tokfr", em.c("tokfr")),
                                  ("hve", max(0, em.c("hve") - 3))]
                            if j == 0 and th == 0:
                                wl.append(("yve", em.c("yve")))
                        em.op("tensor", lambda t, wb=wb, dc=dc, th=th, bank=bank: t.matmul(
                            bank[:, 0:c.THW], wb[:, dc * 128:(dc + 1) * 128],
                            self.tokfm[:, dc * c.Tq + th * c.THW:][:, 0:c.THW],
                            start=(dc == 0), stop=(dc == c.DCH - 1)), waits=wl, inc="ffmm")
                    hv = hgrp[:, jj * c.Tq + th * c.THW:][:, 0:c.THW]
                    em.op("scalar", lambda s, hv=hv, bank=bank: s.activation(
                        hv, bank[:, 0:c.THW], AF.Gelu),
                        waits=[("ffmm", em.c("ffmm"))], inc="hve")
            for tl in range(c.NQC):
                for ds in range(c.NDS):
                    bank = self.ps[4 + (tl * c.NDS + ds) % 4]
                    for jj in range(c.JGRP):
                        wl = []
                        if jj == 0:
                            wl = [("hve", em.c("hve")), ("w2l0", em.c("w2l0")),
                                  ("w2l1", em.c("w2l1")),
                                  ("yve", max(0, em.c("yve") - 3))]
                        em.op("tensor", lambda t, jj=jj, tl=tl, ds=ds, bank=bank: t.matmul(
                            bank[:, 0:c.DSW],
                            hgrp[:, jj * c.Tq + tl * 128:][:, 0:128],
                            w2b[:, jj * c.D + ds * c.DSW:][:, 0:c.DSW],
                            start=(jj == 0), stop=(jj == c.JGRP - 1)), waits=wl, inc="ffmm2")
                    yv = yacc[:, tl * c.D + ds * c.DSW:][:, 0:c.DSW]
                    em.op("vector", lambda v, yv=yv, bank=bank: v.tensor_tensor(
                        yv, yv, bank[:, 0:c.DSW], AL.add),
                        waits=[("ffmm2", em.c("ffmm2"))], inc="yve")
        ab.freebuf("hgrp")
        ab.freebuf("w1b")
        ab.freebuf("w2b")
        ab.freebuf("tokfm")
        ybf = ab.alloc("ybf", 2 * c.D)
        for tl in range(c.NQC):
            yb = ybf[:, (tl % 2) * c.D:][:, 0:c.D]
            self.vop(lambda v, tl=tl, yb=yb: v.tensor_copy(yb, yacc[:, tl * c.D:(tl + 1) * c.D]),
                     waits=[("yve", em.c("yve"))] + (
                         [self.slw("ylw", tl - 2)] if tl >= 2 else []), inc="yve")
            self.dma_sl("ylw", tl, self.dsc["YLOC"][tl * 128:(tl + 1) * 128, :], yb,
                        waits=[("yve", em.c("yve"))])
        ab.freebuf("ybf")
        af.freebuf("yacc")
        tpc = c.AGCH // 128
        for cch in range(c.NAGC):
            rows = c.n_cores * c.AGCH
            em.op("gpsimd", lambda g, cch=cch, rows=rows: g.collective_compute(
                "AllGather", AL.bypass, replica_groups=[list(range(c.n_cores))],
                ins=[self.dsc["YLOC"][cch * c.AGCH:(cch + 1) * c.AGCH, :]],
                outs=[self.dsc["YALLC"][cch * rows:(cch + 1) * rows, :]]),
                waits=[self.slw("ylw", cch * tpc + i) for i in range(tpc)], inc="cc")

    def _p7(self):
        self.soft_barrier()
        c, em, ab, af = self.cfg, self.em, self.ab, self.af
        x1b = af.alloc("x1b", 2 * c.D)
        cg = ab.alloc("cgath", 4 * c.D)
        nsc = 2 * c.NQC
        srm = af.alloc("p7rm", 5 * nsc)
        yidx = self.ai.alloc("yidx", nsc)

        def R7(fn, waits=None):
            em.op("vector", fn, waits=waits if waits is not None else
                  [("ove", em.c("ove"))], inc="ove")
        sall = srm[:, 0:nsc]
        for k in range(c.NQC):
            for j in range(2):
                sv = self.myrec[:, 4 * k + 2 * j:4 * k + 2 * j + 1]
                R7(lambda v, sv=sv, k=k, j=j: v.tensor_scalar(
                    sall[:, 2 * k + j:2 * k + j + 1], sv, 0.0, None, AL.max),
                   waits=[("ind", em.c("ind")), ("ove", em.c("ove"))])
        R7(lambda v: v.tensor_scalar(sall, sall, float(c.N - 1), None, AL.min))
        self._ag_remap(srm[:, nsc:2 * nsc], sall, srm[:, 2 * nsc:5 * nsc],
                       yidx[:, :], R7)
        R7(lambda v: v.tensor_copy(yidx[:, :], srm[:, nsc:2 * nsc]))
        ove_remap = em.c("ove")
        ove_comb = {}
        for k in range(c.NQC):
            sl = k % 2
            xb = x1b[:, sl * c.D:][:, 0:c.D]
            cgv = cg[:, sl * 2 * c.D:][:, 0:2 * c.D]
            self.dma_sl("x1rl", k, xb, self.dsc["X1PARK"][k * 128:(k + 1) * 128, :],
                        waits=[(f"outw{k % 2}", 16 * (k // 2)) if k >= 2 else
                               ("x1pw", em.c("x1pw"))])
            gslot_guard = ove_comb[k - 2] if k >= 2 else ove_remap
            for j in range(2):
                em.op("gpsimd", lambda g, j=j, k=k, cgv=cgv: g.indirect_dma_start(
                    out=cgv[:, j * c.D:(j + 1) * c.D], out_offset=None,
                    in_=self.dsc["YALLC"],
                    in_offset=IndirectOffsetOnAxis(
                        ap=yidx[:, 2 * k + j:2 * k + j + 1].bitcast(U32), axis=0)),
                    waits=[("ove", gslot_guard), ("cc", em.c("cc"))],
                    inc=f"p7g{k % 2}", dma=True)
            q0 = self.myrec[:, 4 * k + 1:4 * k + 2]
            q1 = self.myrec[:, 4 * k + 3:4 * k + 4]
            self.vop(lambda v, q0=q0, xb=xb, cgv=cgv: v.scalar_tensor_tensor(
                xb, cgv[:, 0:c.D], q0, xb, AL.mult, AL.add),
                waits=[(f"p7g{k % 2}", 32 * (k // 2 + 1)), self.slw("x1rl", k)],
                inc="ove")
            self.vop(lambda v, q1=q1, xb=xb, cgv=cgv: v.scalar_tensor_tensor(
                xb, cgv[:, c.D:2 * c.D], q1, xb, AL.mult, AL.add),
                waits=[("ove", em.c("ove"))], inc="ove")
            ove_comb[k] = em.c("ove")
            self.dma(self.douts["out"][k * 128:(k + 1) * 128, :], xb,
                     waits=[("ove", em.c("ove"))], inc=f"outw{k % 2}")
        af.freebuf("x1b")
        af.freebuf("p7rm")
        ab.freebuf("cgath")

    def _taps(self):
        em = self.em
        allw = [(s, em.c(s)) for s in list(em.cnt.keys())
                if em.c(s) > 0 and not s.startswith("outw")]
        for t in self.taps:
            sname, shape, dt = self.tapspec[t]
            self.dma(self.douts["tap_" + t], self.dsc[sname], waits=allw, inc="outw")

    def _final(self):
        em = self.em
        waits = [(s, em.c(s)) for s in list(em.cnt.keys()) if em.c(s) > 0]
        em.op("sync", lambda s: None, waits=waits)

    def host_in_maps(self, inputs):
        c = self.cfg
        x = np.asarray(inputs["x"], np.float32).reshape(c.B, c.T, c.D)
        wqT = np.ascontiguousarray(np.asarray(inputs["wq"], np.float32).T)
        wkT = np.ascontiguousarray(np.asarray(inputs["wk"], np.float32).T)
        wvT = np.ascontiguousarray(np.asarray(inputs["wv"], np.float32).T)
        woT = np.ascontiguousarray(np.asarray(inputs["wo"], np.float32).T)
        gq = np.asarray(inputs["gamma_q"], np.float32)
        gk = np.asarray(inputs["gamma_k"], np.float32)
        Wg = np.ascontiguousarray(np.asarray(inputs["Wg"], np.float32))
        bg = np.asarray(inputs["bg"], np.float32)
        W1 = np.asarray(inputs["W1"], np.float32)
        W2 = np.asarray(inputs["W2"], np.float32)
        gqrep = np.tile(gq, 2)[None, :].astype(np.float32)
        gkrep = np.tile(gk, 2)[None, :].astype(np.float32)
        bgrep = np.ascontiguousarray(np.broadcast_to(bg[None, :], (128, c.E)).astype(np.float32))
        maps = []
        for r in range(c.n_cores):
            b, half = r // 2, r % 2
            t0 = half * c.Tq
            lo = t0 - c.Tq
            slab = np.zeros((c.S, c.D), np.float32)
            src_lo = max(lo, 0)
            slab[src_lo - lo:] = x[b, src_lo:t0 + c.Tq]
            kvalid = np.zeros((128, c.NSC), np.float32)
            for scb in range(c.NSC):
                rows = lo + scb * 128 + np.arange(128)
                kvalid[:, scb] = (rows >= 0).astype(np.float32)
            e = r // 2
            W1H = np.ascontiguousarray(
                W1[e].reshape(c.DCH, 128, c.NH1, c.PCW).transpose(2, 0, 1, 3).reshape(
                    c.NW1U, 128, c.PCW))
            W2H = np.ascontiguousarray(W2[e].reshape(c.NJ, 128, c.D))
            maps.append(dict(
                x_slab=slab, wqT=wqT, wkT=wkT, wvT=wvT, woT=woT,
                gqrep=gqrep, gkrep=gkrep, Wg=Wg, bgrep=bgrep,
                W1H=W1H, W2H=W2H,
                t0col=np.full((128, 1), float(lo), np.float32),
                rankcol=np.full((128, 1), float(r), np.float32),
                hcol=np.full((128, 1), float(half), np.float32),
                ecol=np.full((128, 1), float(e), np.float32),
                kvalid=kvalid,
            ))
        return maps

    def assemble(self, results):
        c = self.cfg
        return np.concatenate([np.asarray(r["out"]) for r in results], axis=0).reshape(
            c.B, c.T, c.D)


def kernel(L0_attn_x, L0_attn_wq, L0_attn_wk, L0_attn_wv, L0_attn_wo,
           L0_attn_gamma_q, L0_attn_gamma_k, L0_ffn_Wg, L0_ffn_bg,
           L0_ffn_W1, L0_ffn_W2):
    cfg = Cfg.full()
    mk = MK(cfg)
    nc = mk.build()
    maps = mk.host_in_maps(dict(
        x=L0_attn_x, wq=L0_attn_wq, wk=L0_attn_wk, wv=L0_attn_wv, wo=L0_attn_wo,
        gamma_q=L0_attn_gamma_q, gamma_k=L0_attn_gamma_k,
        Wg=L0_ffn_Wg, bg=L0_ffn_bg, W1=L0_ffn_W1, W2=L0_ffn_W2))
    res = run_bass_kernel_spmd(nc, maps, core_ids=list(range(cfg.n_cores)))
    out = mk.assemble(res.results)
    return np.ascontiguousarray(out.astype(np.float32))
